# version 01 (final)
# baseline (speedup 1.0000x reference)
import os
import numpy as np
import ml_dtypes
import concourse.bass as bass
import concourse.mybir as mybir
from concourse.bass_utils import run_bass_kernel_spmd
from contextlib import ExitStack

F32 = mybir.dt.float32
BF16 = mybir.dt.bfloat16
I32 = mybir.dt.int32
AF = mybir.ActivationFunctionType
ALU = mybir.AluOpType
AX = mybir.AxisListType
BF = ml_dtypes.bfloat16

S = 2048
D = 1024
DT = 8
NCORE = 8

N_SUB = int(os.environ.get("KDEV_NSUB", "6"))


class Sched:

    def __init__(self, nc, es):
        self.nc = nc
        self.ops = {e: [] for e in ("te", "ve", "se", "gp", "sq")}
        self.cnt = {s: 0 for s in ("te", "ve", "se", "gp", "gq", "sq")}
        self.sem = {s: es.enter_context(nc.semaphore("sem_" + s))
                    for s in self.cnt}
        self.wm = {e: {} for e in self.ops}
        self.last_gp = None

    def _resolve(self, eng, deps, same_ok=True):
        out = []
        for d in deps:
            if d is None:
                continue
            sname, val = d
            if sname == eng and same_ok:
                continue
            w = self.wm[eng].get(sname, 0)
            if val > w:
                out.append((self.sem[sname], val))
                self.wm[eng][sname] = val
        return out

    def I(self, eng, fn, *args, deps=(), inc=False, **kw):
        if eng == "gp":
            inc = True
            deps = list(deps) + [self.last_gp]
        waits = self._resolve(eng, deps, same_ok=(eng != "gp"))
        tok = None
        if inc:
            self.cnt[eng] += 1
            tok = (eng, self.cnt[eng])
        sem = self.sem[eng]

        def thunk(e, waits=waits, fn=fn, args=args, kw=kw, inc=inc, sem=sem):
            for s, v in waits:
                e.wait_ge(s, v)
            ins = getattr(e, fn)(*args, **kw)
            if inc:
                ins.then_inc(sem, 1)
        self.ops[eng].append(thunk)
        if eng == "gp":
            self.last_gp = tok
        return tok

    def dma(self, queue, dst, src, deps=(), **kw):
        sname = "sq" if queue == "sq" else "gq"
        waits = self._resolve(queue, deps)
        self.cnt[sname] += 16
        tok = (sname, self.cnt[sname])
        sem = self.sem[sname]

        def thunk(e, waits=waits, dst=dst, src=src, kw=kw, sem=sem):
            for s, v in waits:
                e.wait_ge(s, v)
            e.dma_start(dst, src, **kw).then_inc(sem, 16)
        self.ops[queue].append(thunk)
        return tok

    def mm(self, out, lhsT, rhs, start, stop, deps=(), inc=False,
           tile_position=None):
        kw = {}
        if tile_position is not None:
            kw["tile_position"] = tile_position
        return self.I("te", "matmul", out, lhsT, rhs, start=start, stop=stop,
                      deps=deps, inc=inc, **kw)

    def now(self, eng):
        return (eng, self.cnt[eng])

    def finalize(self):
        nc = self.nc
        with nc.Block() as blk:
            @blk.tensor
            def _(e):
                for f in self.ops["te"]:
                    f(e)

            @blk.vector
            def _(e):
                for f in self.ops["ve"]:
                    f(e)

            @blk.scalar
            def _(e):
                for f in self.ops["se"]:
                    f(e)

            @blk.gpsimd
            def _(e):
                for f in self.ops["gp"]:
                    f(e)

            @blk.sync
            def _(e):
                for f in self.ops["sq"]:
                    f(e)


def build_program(n_sub):
    nc = bass.Bass("TRN2", target_bir_lowering=False,
                   detect_race_conditions=False)
    es = ExitStack()
    sch = Sched(nc, es)
    I, dma, mm = sch.I, sch.dma, sch.mm

    def dram(name, shape, dt=F32, kind="ExternalInput"):
        return nc.dram_tensor(name, shape, dt, kind=kind)

    sbuf = nc.alloc_sbuf_tensor
    psum_a = nc.alloc_psum_tensor

    x_in = dram("x_in", [DT, 128, S])
    x_out = dram("x_out", [DT, 128, S], kind="ExternalOutput")
    d_w1 = dram("d_w1", [16, 128, 2048])
    d_w2 = dram("d_w2", [16, 128, 2048])
    d_b1 = dram("d_b1", [128, 32])
    d_b2 = dram("d_b2", [128, 8])
    d_wqkv0 = dram("d_wqkv0", [3, 8, 128, 1024])
    d_bqk0 = dram("d_bqk0", [2, 128, 8])
    d_bv0 = dram("d_bv0", [1, 1024])
    d_wo0 = dram("d_wo0", [8, 128, 1024])
    d_bo0 = dram("d_bo0", [128, 8])
    d_maskB0 = dram("d_maskB0", [128, 896])
    d_wqkv1 = dram("d_wqkv1", [3, 8, 128, 512])
    d_wo1 = dram("d_wo1", [4, 128, 1024])
    d_g1b1 = dram("d_g1b1", [2, 128, 8])
    d_maskA1 = dram("d_maskA1", [128, 896])
    d_maskB1 = dram("d_maskB1", [128, 896])
    d_gw = dram("d_gw", [DT, 128, 8])
    d_we1 = dram("d_we1", [8, DT, 128, 1024])
    d_we2 = dram("d_we2", [8, DT, 128, 1024])
    d_we3 = dram("d_we3", [8, DT, 128, 1024])
    d_be1 = dram("d_be1", [8, 128, 8])
    d_be2 = dram("d_be2", [8, 128, 8])
    d_be3 = dram("d_be3", [8, 128, 8])
    d_sw1 = dram("d_sw1", [DT, 128, 2048])
    d_sw2 = dram("d_sw2", [DT, 128, 2048])
    d_sw3 = dram("d_sw3", [16, 128, 1024])
    s_w1q = dram("s_w1q", [16, 128, 2048], BF16, kind="Internal")
    s_w2q = dram("s_w2q", [16, 128, 2048], BF16, kind="Internal")
    dbg_out = dram("dbg_out", [DT, 128, S], F32, kind="ExternalOutput")
    dbg2 = dram("dbg2", [DT, 128, S], F32, kind="ExternalOutput")
    dbg3 = dram("dbg3", [DT, 128, S], F32, kind="ExternalOutput")
    DBG = os.environ.get("KDEV_DEBUG", "")

    x = sbuf("x", [128, DT, S], F32)
    xn = sbuf("xn", [128, DT, S], BF16)
    ones = sbuf("ones", [128, 128], BF16)
    ones_f = sbuf("ones_f", [1, 128], F32)
    rows = sbuf("rows", [1, 3 * 512], F32)
    ident = sbuf("ident", [128, 128], F32)
    consts = sbuf("consts", [1, 8], F32)
    b1q = sbuf("b1q", [128, 64], F32)
    b2q = sbuf("b2q", [128, 64], F32)
    lnc = sbuf("lnc", [128, 1, 512], BF16)
    lnsq = sbuf("lnsq", [128, 1, 512], BF16)
    lnlo = sbuf("lnlo", [128, 1, 512], BF16)
    lnpx = sbuf("lnpx", [128, 512], F32)

    p_mm = [psum_a(f"p_mm{i}", [128, 512], F32) for i in range(4)]
    p_st = [psum_a(f"p_st{i}", [128, 512], F32) for i in range(2)]
    p_bc = [psum_a(f"p_bc{i}", [128, 512], F32) for i in range(2)]

    for dt_i in range(DT):
        T_X = dma("sq", x[:, dt_i, :], x_in.ap()[dt_i])
    I("gp", "memset", ones[:, :], 1.0, inc=True)
    I("gp", "memset", ones_f[:, :], 1.0, inc=True)
    I("gp", "memset", ident[:, :], 1.0, inc=True)
    T_ID = I("gp", "affine_select", out=ident[:, :], in_=ident[:, :],
             compare_op=ALU.is_equal, fill=0.0, base=0,
             pattern=[[-1, 128]], channel_multiplier=1, inc=True)
    T_CONST = I("gp", "memset", consts[0:1, 0:1], 1e-5, inc=True)

    G = {"pst": None, "pbc": None, "q": None}

    def bar():
        return [sch.now(e) for e in ("te", "ve", "se", "gp", "gq", "sq")]

    scope_stack = []

    def scope_open():
        scope_stack.append(ExitStack())
        return scope_stack[-1]

    _uniq = [0]

    def salloc(scope, name, shape, dt):
        _uniq[0] += 1
        return scope.enter_context(
            nc.sbuf_tensor(f"{name}_{_uniq[0]}", shape, dt))

    def scope_close():
        scope_stack.pop().close()

    A = {}

    def quant_rne(fsrc_ap, out_ap, n, scale_in, scale_out, deps):
        z, f, c = A["qa"][:, :n], A["qb"][:, :n], A["qc"][:, :n]
        ii = A["qi"][:, :n]
        d0 = list(deps) + ([G["q"]] if G["q"] else [])
        I("ve", "tensor_scalar", z, fsrc_ap, float(scale_in), None, ALU.mult,
          deps=d0)
        I("ve", "tensor_copy", ii, z)
        I("ve", "tensor_copy", f, ii)
        I("ve", "tensor_tensor", c, f, z, ALU.subtract)
        I("ve", "tensor_scalar", z, c, 0.5, None, ALU.is_gt)
        I("ve", "tensor_tensor", f, f, z, ALU.subtract)
        I("ve", "tensor_scalar", z, c, -0.5, None, ALU.is_lt)
        I("ve", "tensor_tensor", f, f, z, ALU.add)
        I("ve", "tensor_scalar", f, f, -128.0, 127.0, ALU.max, ALU.min)
        tok = I("ve", "tensor_scalar", out_ap, f, float(scale_out), None,
                ALU.mult, inc=True)
        G["q"] = tok
        return tok

    def ln_stats(w, deps, rscale=1.0, src_bf=None, slot=0, hilo=False):
        t0 = 512 * w
        m = rows[:, 0:512]
        v = rows[:, 512:1024]
        r = rows[:, 1024:1536]
        tok_s = None
        tok_m = None
        prev_mm = {}
        for dt_i in range(DT):
            first, last = dt_i == 0, dt_i == DT - 1
            sl = dt_i % 2
            xs = x[:, dt_i, t0:t0 + 512]
            if src_bf is None:
                tc = I("ve", "tensor_copy", lnc[:, 0, :], xs,
                       deps=list(deps) + [prev_mm.get(("m", 0)),
                                          prev_mm.get(("sq", 0))], inc=True)
                chunk = lnc[:, 0, :]
                dchunk = [tc]
            else:
                chunk = src_bf(dt_i)
                dchunk = list(deps)
            pdep = [G["pst"]] if first else []
            if hilo:
                tlo = I("ve", "tensor_tensor", lnlo[:, 0, :], xs, chunk,
                        ALU.subtract, deps=dchunk + [prev_mm.get(("lo", 0))],
                        inc=True)
                tpx = I("ve", "tensor_tensor", lnpx[:, :], xs, xs, ALU.mult,
                        inc=True)
                th2 = I("ve", "tensor_copy", lnsq[:, 0, :], lnpx[:, :],
                        deps=[prev_mm.get(("sq", 0))], inc=True)
                tm1 = mm(p_st[0][0:1, :], ones[:, 0:1], chunk, first, False,
                         deps=dchunk + pdep, inc=True)
                tm2 = mm(p_st[0][0:1, :], ones[:, 0:1], lnlo[:, 0, :], False,
                         last, deps=[tlo], inc=True)
                tq1 = mm(p_st[1][0:1, :], ones[:, 0:1], lnsq[:, 0, :], first,
                         False, deps=[th2] + pdep, inc=True)
                tl2 = I("ve", "tensor_tensor", lnlo[:, 0, :], lnpx[:, :],
                        lnsq[:, 0, :], ALU.subtract, deps=[tm2], inc=True)
                tok_s = mm(p_st[1][0:1, :], ones[:, 0:1], lnlo[:, 0, :], False,
                           last, deps=[tl2], inc=True)
                prev_mm[("m", 0)] = tm1
                prev_mm[("sq", 0)] = tq1
                prev_mm[("lo", 0)] = tok_s
            else:
                tq = I("se", "activation", lnsq[:, 0, :], chunk, AF.Square,
                       deps=dchunk + [prev_mm.get(("sq", 0))], inc=True)
                tm1 = mm(p_st[0][0:1, :], ones[:, 0:1], chunk, first, last,
                         deps=dchunk + pdep, inc=True)
                tok_s = mm(p_st[1][0:1, :], ones[:, 0:1], lnsq[:, 0, :], first,
                           last, deps=[tq], inc=True)
                prev_mm[("m", 0)] = tm1
                prev_mm[("sq", 0)] = tok_s
        t1 = I("ve", "tensor_scalar", m, p_st[0][0:1, :512], 1.0 / D, None,
               ALU.mult, deps=[tok_s, G.get("rows")], inc=True)
        t2 = I("ve", "tensor_scalar", v, p_st[1][0:1, :512], 1.0 / D, None,
               ALU.mult, deps=[tok_s], inc=True)
        G["pst"] = t2
        I("ve", "tensor_tensor", r, m, m, ALU.mult)
        tv = I("ve", "tensor_tensor", v, v, r, ALU.subtract, inc=True)
        tve = I("ve", "tensor_scalar", v, v, 1e-5, None, ALU.add, deps=[tv],
                inc=True)
        tl = I("se", "activation", r, v, AF.Ln, deps=[tve], inc=True)
        tr0 = I("se", "activation", r, r, AF.Exp, scale=-0.5, deps=[tl],
                inc=True)
        tmp = lnpx[0:1, :]
        I("ve", "tensor_tensor", tmp, r, r, ALU.mult, deps=[tr0])
        I("ve", "tensor_tensor", tmp, tmp, v, ALU.mult)
        I("ve", "tensor_scalar", tmp, tmp, -0.5, 1.5, ALU.mult, ALU.add)
        tr = I("ve", "tensor_tensor", r, r, tmp, ALU.mult, inc=True)
        if rscale != 1.0:
            trb = I("ve", "tensor_scalar", v, r, float(rscale), None,
                    ALU.mult, deps=[tr], inc=True)
            rsc = v
        else:
            trb = tr
            rsc = r
        return dict(m=m, v=v, r=r, rsc=rsc, t_m=t1, t_r=tr, t_rb=trb)

    def bcast2(st):
        dep_g = [G["pbc"]] if G["pbc"] else []
        tb0 = mm(p_bc[0][:, :], ones_f[0:1, :], st["m"], True, True,
                 deps=[st["t_m"]] + dep_g, inc=True)
        tb1 = mm(p_bc[1][:, :], ones_f[0:1, :], st["rsc"], True, True,
                 deps=[st["t_rb"]] + dep_g, inc=True)
        G["rows"] = tb1
        return tb0, tb1

    def alloc_arena(sc):
        for nm, dt_ in (("qa", F32), ("qb", F32), ("qc", F32)):
            A[nm] = salloc(sc, nm, [128, 2048], dt_)
        A["qi"] = salloc(sc, "qi", [128, 2048], I32)
        A["qj"] = salloc(sc, "qj", [128, 2048], I32)

    sc0 = scope_open()
    alloc_arena(sc0)
    wq_f = salloc(sc0, "wq_f", [128, 2048], F32)
    wq_b = salloc(sc0, "wq_b", [128, 2048], BF16)
    t_w1q, t_w2q = [], []
    t_chain = None
    t_store = None
    for j in range(32):
        srcj = (d_w1 if j < 16 else d_w2).ap()[j % 16]
        dst = (s_w1q if j < 16 else s_w2q).ap()[j % 16]
        tl = dma("sq", wq_f[:, :], srcj,
                 deps=[t_chain] if t_chain else [])
        tq = quant_rne(wq_f[:, :], wq_b[:, :], 2048, 64.0, 1.0 / 1024.0,
                       deps=[tl, t_store])
        t_chain = tq
        ts_ = dma("gp", dst, wq_b[:, :], deps=[tq])
        t_store = ts_
        (t_w1q if j < 16 else t_w2q).append(ts_)
    tz = I("ve", "memset", wq_f[:, 0:64], 0.0, deps=[t_chain], inc=True)
    t_bl = dma("sq", wq_f[:, 0:32], d_b1.ap(), deps=[tz])
    t_b1 = quant_rne(wq_f[:, 0:64], b1q[:, 0:64], 64, 64.0, 16.0,
                     deps=[t_bl])
    tz2 = I("ve", "memset", wq_f[:, 64:128], 0.0, deps=[t_b1], inc=True)
    t_bl2 = dma("sq", wq_f[:, 64:72], d_b2.ap(), deps=[tz2])
    t_b2 = quant_rne(wq_f[:, 64:128], b2q[:, 0:64], 64, 64.0, 1.0 / 128.0,
                     deps=[t_bl2])

    def l0ffn(sc):
        B0 = bar()
        w1buf = salloc(sc, "w1buf", [128, 8, 1024], BF16)
        w2buf = salloc(sc, "w2buf", [128, 8, 1024], BF16)
        hq = salloc(sc, "hq", [128, 2048], BF16)
        kf = salloc(sc, "kf", [128, 2048], F32)
        t_xq = [None] * 4
        for w in range(4):
            st = ln_stats(w, deps=B0 + [T_X], rscale=16.0, slot=w % 2,
                          hilo=True)
            tb0, tb1 = bcast2(st)
            last = None
            for dt_i in range(DT):
                xs = x[:, dt_i, 512 * w:512 * w + 512]
                za, f, c = A["qa"][:, :512], A["qb"][:, :512], A["qc"][:, :512]
                ii = A["qi"][:, :512]
                I("ve", "tensor_tensor", za, xs, p_bc[0][:, :], ALU.subtract,
                  deps=[tb0, T_X] + ([G["q"]] if G["q"] and dt_i == 0 else []))
                I("ve", "tensor_tensor", za, za, p_bc[1][:, :], ALU.mult,
                  deps=[tb1])
                I("ve", "tensor_copy", ii, za)
                I("ve", "tensor_copy", f, ii)
                I("ve", "tensor_tensor", c, f, za, ALU.subtract)
                I("ve", "tensor_scalar", za, c, 0.5, None, ALU.is_gt)
                I("ve", "tensor_tensor", f, f, za, ALU.subtract)
                I("ve", "tensor_scalar", za, c, -0.5, None, ALU.is_lt)
                I("ve", "tensor_tensor", f, f, za, ALU.add)
                I("ve", "tensor_scalar", f, f, -128.0, 127.0, ALU.max, ALU.min)
                last = I("ve", "tensor_copy",
                         xn[:, dt_i, 512 * w:512 * w + 512], f, inc=True)
            t_xq[w] = last
            G["q"] = last
            G["pbc"] = last
        if DBG == "w1q":
            for j in range(8):
                dma("gp", dbg_out.ap()[j], s_w1q.ap()[j][:, 0:2048],
                    deps=[t_w1q[15]])
        if DBG == "xq":
            for dt_i in range(DT):
                dma("gp", dbg_out.ap()[dt_i], xn[:, dt_i, :],
                    deps=[t_xq[3], sch.now("ve")])
        t_badd = None
        for dout in range(DT):
            t_badd = I("ve", "tensor_scalar", x[:, dout, :], x[:, dout, :],
                       b2q[:, dout:dout + 1], None, ALU.add,
                       deps=[t_b2], inc=True)
        pg = {i: None for i in range(4)}
        t_w1_load = t_w2_load = None
        t_ph1_last = t_ph2_last = None
        t_hq_chain = None
        for hc in range(4):
            dep1 = [t_ph1_last] + B0
            for k in range(DT):
                half, coff = hc // 2, (hc % 2) * 1024
                t_w1_load = dma("sq", w1buf[:, k, :],
                                s_w1q.ap()[k * 2 + half][:, coff:coff + 1024],
                                deps=[t_w1q[k * 2 + half]] + dep1)
            dep2 = [t_ph2_last] + B0
            for k in range(DT):
                jj, coff = hc * 4 + k // 2, (k % 2) * 1024
                t_w2_load = dma("sq", w2buf[:, k, :],
                                s_w2q.ap()[jj][:, coff:coff + 1024],
                                deps=[t_w2q[jj]] + dep2)
            for tt in range(8):
                ts0 = 256 * tt
                w = ts0 // 512
                se_toks = []
                for m in range(8):
                    pmi = m % 2
                    pm = p_mm[pmi]
                    for k in range(DT):
                        d = ([t_w1_load, t_xq[w], pg[pmi]] if k == 0 else ())
                        tok = mm(pm[:, :256],
                                 w1buf[:, k, m * 128:(m + 1) * 128],
                                 xn[:, k, ts0:ts0 + 256], k == 0, k == DT - 1,
                                 deps=d, inc=(k == DT - 1))
                    ht = hc * 8 + m
                    tse = I("se", "activation", kf[:, m * 256:(m + 1) * 256],
                            pm[:, :256], AF.Relu, bias=b1q[:, ht:ht + 1],
                            scale=1024.0, deps=[tok, t_b2, t_hq_chain],
                            inc=True)
                    pg[pmi] = tse
                    se_toks.append(tse)
                    t_ph1_last = tok
                I("ve", "tensor_copy", A["qi"][:, :2048], kf[:, :2048],
                  deps=se_toks + [G["q"]])
                I("ve", "tensor_scalar", A["qj"][:, :2048], A["qi"][:, :2048], 6, None,
                  ALU.arith_shift_right)
                I("ve", "tensor_scalar", A["qj"][:, :2048], A["qj"][:, :2048], 1, None,
                  ALU.bitwise_and)
                I("ve", "tensor_tensor", A["qi"][:, :2048], A["qi"][:, :2048],
                  A["qj"][:, :2048], ALU.add)
                I("ve", "tensor_scalar", A["qi"][:, :2048], A["qi"][:, :2048], 31, None,
                  ALU.add)
                I("ve", "tensor_scalar", A["qi"][:, :2048], A["qi"][:, :2048], 6, None,
                  ALU.arith_shift_right)
                t_hq_chain = I("ve", "tensor_scalar", hq[:, :2048],
                               A["qi"][:, :2048], 127.0, None, ALU.min,
                               deps=[t_ph2_last], inc=True)
                G["q"] = t_hq_chain
                if DBG == "hq" and hc == 0 and tt == 0:
                    dma("sq", dbg_out.ap()[2][:, 0:32], b1q[:, :],
                        deps=[t_b1])
                    dma("gp", dbg_out.ap()[0][:, 0:2048], hq[:, :2048],
                        deps=[t_hq_chain])
                    dma("sq", dbg_out.ap()[1][:, 0:2048], kf[:, :2048],
                        deps=[t_hq_chain])
                for dout in range(8):
                    pmi = 2 + dout % 2
                    pm = p_mm[pmi]
                    for k in range(DT):
                        d = ([t_w2_load, t_hq_chain, pg[pmi]] if k == 0 else ())
                        tok2 = mm(pm[:, :256],
                                  w2buf[:, k, dout * 128:(dout + 1) * 128],
                                  hq[:, k * 256:(k + 1) * 256],
                                  k == 0, k == DT - 1,
                                  deps=d, inc=(k == DT - 1))
                    tev = I("ve", "scalar_tensor_tensor",
                            x[:, dout, ts0:ts0 + 256], pm[:, :256], 0.5,
                            x[:, dout, ts0:ts0 + 256], ALU.mult, ALU.add,
                            deps=[tok2], inc=True)
                    pg[pmi] = tev
                    t_ph2_last = tok2
        return sch.now("ve")



    def moe(sc):
        B0 = bar()
        onesc = salloc(sc, "onesc", [128, 1], F32)
        ones8 = salloc(sc, "ones8", [128, 8], F32)
        gwbuf = salloc(sc, "gwbuf", [128, DT, 8], F32)
        se8 = salloc(sc, "se8", [1, 8], F32)
        s8b = salloc(sc, "s8b", [128, 8], F32)
        mr8 = salloc(sc, "mr8", [128, 8], F32)
        pgs = [salloc(sc, f"pg8_{q}", [128, 8], F32) for q in range(4)]
        lgs = [salloc(sc, f"lg8_{q}", [128, 8], F32) for q in range(4)]
        t8s = [salloc(sc, f"t8_{q}", [128, 8], F32) for q in range(4)]
        n1s = [salloc(sc, f"n1_{q}", [128, 1], F32) for q in range(4)]
        n2s = [salloc(sc, f"n2_{q}", [128, 1], F32) for q in range(4)]
        prs = [salloc(sc, f"pr8_{q}", [128, 8], F32) for q in range(4)]
        d8s = [salloc(sc, f"d8_{q}", [128, 1], F32) for q in range(4)]
        dds = [salloc(sc, f"dd8_{q}", [128, 8], F32) for q in range(4)]
        rcs = [salloc(sc, f"rec8_{q}", [128, 8], F32) for q in range(4)]
        mks = [salloc(sc, f"msk8_{q}", [128, 8], F32) for q in range(4)]
        cws = [salloc(sc, f"cw8_{q}", [128, 8], F32) for q in range(4)]
        sp32 = salloc(sc, "sp32", [128, 128], F32)
        cwT = salloc(sc, "cwT", [65, 3 * S], BF16)
        mkT = salloc(sc, "mkT", [65, 3 * S], BF16)
        t_on = I("ve", "memset", onesc[:, :], 1.0, deps=B0, inc=True)
        t_on8 = I("ve", "memset", ones8[:, :], 1.0, deps=B0, inc=True)
        t_sp0 = I("ve", "memset", sp32[:, :], 0.0, deps=B0, inc=True)
        for dt_i in range(DT):
            t_gw = dma("sq", gwbuf[:, dt_i, :], d_gw.ap()[dt_i], deps=B0)
        for dt_i in range(DT):
            tok_se = mm(p_st[0][0:1, 0:8], onesc[:, :], gwbuf[:, dt_i, :],
                        dt_i == 0, dt_i == DT - 1,
                        deps=[t_gw, t_on, G["pst"], B0[0]] if dt_i == 0 else (),
                        inc=(dt_i == DT - 1))
        t_se8 = I("ve", "tensor_copy", se8[:, :], p_st[0][0:1, 0:8],
                  deps=[tok_se], inc=True)
        G["pst"] = t_se8
        tok_sb = mm(p_st[1][:, 0:8], ones_f[0:1, :], se8[0:1, :], True, True,
                    deps=[t_se8, G.get("pst1")], inc=True)
        t_s8b = I("ve", "tensor_copy", s8b[:, :], p_st[1][:, 0:8],
                  deps=[tok_sb], inc=True)
        G["pst1"] = t_s8b
        G["pst"] = t_s8b

        t_xn = [None] * 4
        t_gT = None
        for w in range(4):
            st = ln_stats(w, deps=B0 + [T_X])
            tb0, tb1 = bcast2(st)
            last = None
            for dt_i in range(DT):
                xs = x[:, dt_i, 512 * w:512 * w + 512]
                I("ve", "tensor_tensor", lnpx[:, :], xs, p_bc[0][:, :],
                  ALU.subtract, deps=[tb0])
                last = I("ve", "tensor_tensor",
                         xn[:, dt_i, 512 * w:512 * w + 512],
                         lnpx[:, :], p_bc[1][:, :], ALU.mult, deps=[tb1],
                         inc=True)
            G["pbc"] = last
            t_xn[w] = last
            if DBG == "cw":
                dma("sq", dbg_out.ap()[5][0:1, w * 512:w * 512 + 512],
                    st["m"], deps=[st["t_m"]])
                dma("sq", dbg_out.ap()[6][0:1, w * 512:w * 512 + 512],
                    st["r"], deps=[st["t_r"]])
            tpgt = [None] * 4
            trmt = [None] * 4
            trrt = [None] * 4
            for tc4 in range(4):
                tc = w * 4 + tc4
                for dt_i in range(DT):
                    tok_a = mm(p_st[0][:, 0:8],
                               x[:, dt_i, tc * 128:tc * 128 + 128],
                               gwbuf[:, dt_i, :], dt_i == 0, dt_i == DT - 1,
                               deps=[t_gw, G["pst"], T_X] if dt_i == 0 else (),
                               inc=(dt_i == DT - 1))
                tpgt[tc4] = I("ve", "tensor_copy", pgs[tc4][:, :],
                              p_st[0][:, 0:8], deps=[tok_a], inc=True)
                G["pst"] = tpgt[tc4]
                ttrm = I("te", "transpose", p_st[0][:, 0:1],
                         st["m"][0:1, tc4 * 128:tc4 * 128 + 128],
                         ident[0:1, 0:1],
                         deps=[st["t_m"], T_ID, tpgt[tc4]], inc=True)
                trmt[tc4] = I("ve", "tensor_reduce", mr8[:, tc4:tc4 + 1],
                              p_st[0][:, 0:1], AX.X, ALU.max, negate=True,
                              deps=[ttrm], inc=True)
                G["pst"] = trmt[tc4]
                ttrr = I("te", "transpose", p_st[0][:, 0:1],
                         st["r"][0:1, tc4 * 128:tc4 * 128 + 128],
                         ident[0:1, 0:1],
                         deps=[st["t_r"], trmt[tc4]], inc=True)
                trrt[tc4] = I("ve", "tensor_reduce", mr8[:, 4 + tc4:5 + tc4],
                              p_st[0][:, 0:1], AX.X, ALU.max, deps=[ttrr],
                              inc=True)
                G["pst"] = trrt[tc4]
                G["rows"] = ttrr
            for q in range(4):
                I("ve", "scalar_tensor_tensor", lgs[q][:, :], s8b[:, :],
                  mr8[:, q:q + 1], pgs[q][:, :], ALU.mult, ALU.add,
                  deps=[t_s8b, trmt[q], G.get("dbgcw")])
            for q in range(4):
                I("ve", "tensor_scalar", lgs[q][:, :], lgs[q][:, :],
                  mr8[:, 4 + q:5 + q], None, ALU.mult, deps=[trrt[q]])
            for q in range(4):
                I("ve", "tensor_reduce", n1s[q][:, :], lgs[q][:, :], AX.X,
                  ALU.max, negate=True)
            for q in range(4):
                I("ve", "tensor_scalar", t8s[q][:, :], lgs[q][:, :],
                  n1s[q][:, 0:1], None, ALU.add)
            for q in range(4):
                I("ve", "tensor_scalar", t8s[q][:, :], t8s[q][:, :], 0.0,
                  None, ALU.is_ge)
            for q in range(4):
                I("ve", "scalar_tensor_tensor", t8s[q][:, :], t8s[q][:, :],
                  -1e30, lgs[q][:, :], ALU.mult, ALU.add)
            for q in range(4):
                I("ve", "tensor_reduce", n2s[q][:, :], t8s[q][:, :], AX.X,
                  ALU.max, negate=True)
            for q in range(4):
                I("ve", "tensor_scalar", mks[q][:, :], lgs[q][:, :],
                  n2s[q][:, 0:1], None, ALU.add)
            for q in range(4):
                I("ve", "tensor_scalar", mks[q][:, :], mks[q][:, :], 0.0,
                  None, ALU.is_ge, inc=True)
            vnow = sch.now("ve")
            tprs = [None] * 4
            for q in range(4):
                tprs[q] = I("se", "activation", prs[q][:, :], lgs[q][:, :],
                            AF.Exp, bias=n1s[q][:, :],
                            deps=[vnow] if q == 0 else (), inc=True)
            for q in range(4):
                I("ve", "tensor_reduce", d8s[q][:, :], prs[q][:, :], AX.X,
                  ALU.add, deps=[tprs[q]])
            for q in range(4):
                I("ve", "tensor_scalar", dds[q][:, :], ones8[:, :],
                  d8s[q][:, 0:1], None, ALU.mult, deps=[t_on8])
            for q in range(4):
                I("ve", "reciprocal", rcs[q][:, :], dds[q][:, :])
            for q in range(4):
                I("ve", "tensor_tensor", cws[q][:, :], prs[q][:, :],
                  mks[q][:, :], ALU.mult)
            tcwt = [None] * 4
            for q in range(4):
                tcwt[q] = I("ve", "tensor_tensor", cws[q][:, :], cws[q][:, :],
                            rcs[q][:, :], ALU.mult, inc=True)
            tmkt = [None] * 4
            for q in range(4):
                tmkt[q] = I("ve", "tensor_scalar", mks[q][:, :], mks[q][:, :],
                            0.5, None, ALU.mult, inc=True)
            if DBG == "cw":
                for q in range(4):
                    tc = w * 4 + q
                    G["dbglg"] = dma("sq", dbg_out.ap()[2][:, tc * 8:tc * 8 + 8],
                                     lgs[q][:, :], deps=[tcwt[q]])
                    dma("sq", dbg_out.ap()[3][:, tc * 8:tc * 8 + 8],
                        pgs[q][:, :], deps=[tpgt[q]])
                    dma("sq", dbg_out.ap()[7][:, 256 + tc * 8:256 + tc * 8 + 8],
                        prs[q][:, :], deps=[tprs[q]])
                    G["dbgcw"] = dma("sq",
                                     dbg_out.ap()[7][:, 640 + tc * 8:640 + tc * 8 + 8],
                                     cws[q][:, :], deps=[tcwt[q]])
            for tc4 in range(4):
                tc = w * 4 + tc4
                for half8 in range(2):
                    for srcb, dstT in ((cws[tc4], cwT), (mks[tc4], mkT)):
                        tspr = None
                        for e4 in range(4):
                            e = half8 * 4 + e4
                            tspr = I("ve", "tensor_reduce",
                                     sp32[:, e4 * 32:e4 * 32 + 1],
                                     srcb[:, e:e + 1], AX.X, ALU.max,
                                     deps=[tmkt[tc4], tcwt[tc4],
                                           G.get("sp")], inc=True)
                        ttr = I("te", "transpose", p_st[0][:, 0:128],
                                sp32[:, :], ident[:, :],
                                deps=[tspr, G["pst"], T_ID], inc=True)
                        G["sp"] = ttr
                        tcp = None
                        for e4 in range(4):
                            e = half8 * 4 + e4
                            rw3 = (e % 3) * 32
                            cb3 = (e // 3) * S
                            tcp = I("ve", "tensor_copy",
                                    dstT[rw3:rw3 + 1,
                                         cb3 + tc * 128:cb3 + tc * 128 + 128],
                                    p_st[0][e4 * 32:e4 * 32 + 1, 0:128],
                                    deps=[ttr], inc=True)
                        G["pst"] = tcp
                t_gT = tcp

        if DBG in ("cw", "moe1"):
            for dt_i in range(DT):
                tdx = dma("sq", dbg3.ap()[dt_i], x[:, dt_i, :], deps=B0)
                tdx = dma("gp", dbg2.ap()[dt_i], xn[:, dt_i, :],
                          deps=t_xn + [tdx])
            B0 = B0 + [tdx]
        if DBG == "cw":
            for e in range(8):
                rw3, cb3 = (e % 3) * 32, (e // 3) * S
                dma("gp", dbg_out.ap()[0][e:e + 1, :],
                    cwT[rw3:rw3 + 1, cb3:cb3 + S], deps=[t_gT])
                dma("gp", dbg_out.ap()[1][e:e + 1, :],
                    mkT[rw3:rw3 + 1, cb3:cb3 + S], deps=[t_gT])
        MPART = os.environ.get("KDEV_MPART", "all")
        sce = scope_open()
        w1b = salloc(sce, "mw1", [128, DT, 1024], BF16)
        w2b = salloc(sce, "mw2", [128, DT, 1024], BF16)
        w3b1 = salloc(sce, "mw3", [128, DT, 1024], BF16)
        w3b = [w3b1, w3b1]
        be = [salloc(sce, f"mbe{i}", [128, 3, 8], F32) for i in range(2)]
        x3m = salloc(sce, "mx3m", [128, DT, 512], BF16)
        hb = salloc(sce, "mhb", [128, DT, 256], BF16)
        ut = salloc(sce, "mut", [128, DT, 256], BF16)
        pb = salloc(sce, "mpb", [128, 256], BF16)
        cwb = salloc(sce, "mcwb", [128, 1024], BF16)
        mkb = salloc(sce, "mkb", [128, 1024], BF16)

        def stream_w(dsrc, e, buf, dep0):
            tok = None
            for dt_i in range(DT):
                tok = dma("gp", buf[:, dt_i, :], dsrc.ap()[e][dt_i],
                          deps=dep0 if dt_i == 0 else ())
            return tok

        t_y = None
        g_ub = None
        for half in (range(2) if MPART in ("all", "ge") else ()):
            th0 = half * 1024
            g_w = {"w1": None, "w2": None, "w3": None}
            t_w3 = {}
            t_w3[0] = stream_w(d_we3, 0, w3b[0], [sch.now("te")] + B0)
            g_be = [None, None]
            g_ps = {}
            t_x3m = None
            t_hb = None
            t_pv2 = None
            NEXP = int(os.environ.get("KDEV_NEXP", "8"))
            for e in range(NEXP):
                w3cur = w3b[e % 2]
                becur = be[e % 2]
                QTR = 2
                tbe = dma("sq", becur[:, 0, :], d_be1.ap()[e],
                          deps=[g_be[e % 2]])
                tbe = dma("sq", becur[:, 1, :], d_be2.ap()[e])
                tbe = dma("sq", becur[:, 2, :], d_be3.ap()[e])
                tcwb = tmkb = None
                rw, cb = (e % 3) * 32, (e // 3) * S
                for q2 in range(2):
                    tokc = mm(p_bc[0][:, :], ones[rw:rw + 1, :],
                              cwT[rw:rw + 1,
                                  cb + th0 + q2 * 512:cb + th0 + q2 * 512 + 512],
                              True, True,
                              deps=[t_gT, G.get("mbc0"), G["pbc"]], inc=True)
                    tcwb = I("ve", "tensor_copy",
                             cwb[:, q2 * 512:q2 * 512 + 512], p_bc[0][:, :],
                             deps=[tokc], inc=True)
                    G["mbc0"] = tcwb
                    tokm2 = mm(p_bc[1][:, :], ones[rw:rw + 1, :],
                               mkT[rw:rw + 1,
                                   cb + th0 + q2 * 512:cb + th0 + q2 * 512 + 512],
                               True, True,
                               deps=[t_gT, G.get("mbc1"), G["pbc"]], inc=True)
                    tmkb = I("ve", "tensor_copy",
                             mkb[:, q2 * 512:q2 * 512 + 512], p_bc[1][:, :],
                             deps=[tokm2], inc=True)
                    G["mbc1"] = tmkb
                t_w1 = stream_w(d_we1, e, w1b, [g_w["w1"]] + B0)
                t_w2 = stream_w(d_we2, e, w2b, [g_w["w2"]] + B0)
                for qtr in range(2):
                    tq0 = th0 + qtr * 512
                    for tt in range(2):
                        ts0 = tq0 + tt * 256
                        co = (qtr * 512 + tt * 256)
                        tu = None
                        for dt_i in range(DT):
                            tu = I("ve", "tensor_tensor", ut[:, dt_i, :],
                                   xn[:, dt_i, ts0:ts0 + 256],
                                   cwb[:, co:co + 256], ALU.mult,
                                   deps=[tcwb, t_xn[ts0 // 512], g_ub],
                                   inc=True)
                        for mt in range(DT):
                            pm = p_mm[mt % 2]
                            for dt_i in range(DT):
                                tok3 = mm(pm[:, :256],
                                          w3cur[:, dt_i, mt * 128:mt * 128 + 128],
                                          ut[:, dt_i, :], dt_i == 0,
                                          dt_i == DT - 1,
                                          deps=[tu, t_w3[e],
                                                g_ps.get(("e", mt % 2))]
                                          if dt_i == 0 else (),
                                          inc=(dt_i == DT - 1))
                            t_x3m = I("ve", "scalar_tensor_tensor",
                                      x3m[:, mt, tt * 256:tt * 256 + 256],
                                      pm[:, :256], becur[:, 2, mt:mt + 1],
                                      mkb[:, co:co + 256],
                                      ALU.add, ALU.mult,
                                      deps=[tok3, tbe, tmkb], inc=True)
                            g_ps[("e", mt % 2)] = t_x3m
                        g_ub = tok3
                    if qtr == 1 and e < 7:
                        t_w3[e + 1] = stream_w(d_we3, e + 1, w3b1, [tok3])
                    for tt in range(2):
                        ts0 = tq0 + tt * 256
                        co = (qtr * 512 + tt * 256)
                        tu = None
                        for dt_i in range(DT):
                            tu = I("ve", "tensor_tensor", ut[:, dt_i, :],
                                   xn[:, dt_i, ts0:ts0 + 256],
                                   cwb[:, co:co + 256], ALU.mult,
                                   deps=[tcwb, g_ub], inc=True)
                        for mt in range(DT):
                            pm = p_mm[mt % 2]
                            for dt_i in range(DT):
                                tok1 = mm(pm[:, :256],
                                          w1b[:, dt_i, mt * 128:mt * 128 + 128],
                                          ut[:, dt_i, :], dt_i == 0,
                                          dt_i == DT - 1,
                                          deps=[tu, t_w1,
                                                g_ps.get(("e", mt % 2))]
                                          if dt_i == 0 else (),
                                          inc=(dt_i == DT - 1))
                            t_hb = I("se", "activation", hb[:, mt, :],
                                     pm[:, :256], AF.Silu,
                                     bias=becur[:, 0, mt:mt + 1],
                                     deps=[tok1, tbe, g_ps.get("hb")],
                                     inc=True)
                            g_ps[("e", mt % 2)] = t_hb
                        g_ub = tok1
                        for dout in range(DT):
                            pm = p_mm[2 + dout % 2]
                            for k in range(DT):
                                tok2 = mm(pm[:, :256],
                                          w2b[:, k, dout * 128:dout * 128 + 128],
                                          hb[:, k, :], k == 0, k == DT - 1,
                                          deps=[t_hb, t_w2,
                                                g_ps.get(("o", dout % 2))]
                                          if k == 0 else (),
                                          inc=(k == DT - 1))
                            g_ps["hb"] = tok2
                            tpp = I("ve", "scalar_tensor_tensor", pb[:, :],
                                    pm[:, :256], becur[:, 1, dout:dout + 1],
                                    x3m[:, dout, tt * 256:tt * 256 + 256],
                                    ALU.add, ALU.mult,
                                    deps=[tok2, t_x3m], inc=True)
                            t_pv2 = I("ve", "tensor_tensor",
                                      x[:, dout, ts0:ts0 + 256],
                                      x[:, dout, ts0:ts0 + 256],
                                      pb[:, :], ALU.add, deps=[T_X],
                                      inc=True)
                            g_ps[("o", dout % 2)] = t_pv2
                g_w["w3"] = tok3
                g_w["w1"] = tok1
                g_w["w2"] = tok2
                g_be[e % 2] = t_pv2
                if DBG == "moe1" and e == 0 and half == 0:
                    dv = sch.now("ve")
                    dma("gp", dbg_out.ap()[0][:, 0:1024], cwb[:, :], deps=[dv])
                    dma("gp", dbg_out.ap()[0][:, 1024:2048], mkb[:, :])
                    for j in range(4):
                        dma("gp", dbg_out.ap()[1 + j // 2]
                            [:, (j % 2) * 1024:(j % 2) * 1024 + 1024],
                            x3m[:, 2 * j:2 * j + 2, :])
                    dma("gp", dbg_out.ap()[3], hb[:, :, :])
                    dma("gp", dbg_out.ap()[4], ut[:, :, :])
                    for j in range(3):
                        dma("gp", dbg_out.ap()[5][:, j * 8:j * 8 + 8],
                            becur[:, j, :])
                    dma("gp", dbg_out.ap()[6][:, 0:1024], w1b[:, 0, :])
                    dma("gp", dbg_out.ap()[6][:, 1024:2048], w2b[:, 0, :])
                    dma("gp", dbg_out.ap()[7][:, 0:1024], w3cur[:, 0, :])
        scope_close()

        scs = scope_open()
        sw1b = salloc(scs, "msw1", [128, DT, 1024], BF16)
        sw2b = salloc(scs, "msw2", [128, DT, 1024], BF16)
        sw3b = salloc(scs, "msw3", [128, DT, 1024], BF16)
        h1b = salloc(scs, "mh1", [128, DT, 256], BF16)
        gb = salloc(scs, "mgb", [128, DT, 256], BF16)
        bsh = bar()
        g_sw = {}
        t_sh = None
        for ih in (range(2) if MPART in ("all", "gs") else ()):
            tok_s1 = tok_s2 = tok_s3 = None
            for dt_i in range(DT):
                tok_s1 = dma("gp", sw1b[:, dt_i, :],
                             d_sw1.ap()[dt_i][:, ih * 1024:ih * 1024 + 1024],
                             deps=[g_sw.get("s1")] + bsh if dt_i == 0 else ())
                tok_s2 = dma("gp", sw2b[:, dt_i, :],
                             d_sw2.ap()[dt_i][:, ih * 1024:ih * 1024 + 1024],
                             deps=[g_sw.get("s2")] if dt_i == 0 else ())
                tok_s3 = dma("gp", sw3b[:, dt_i, :],
                             d_sw3.ap()[ih * 8 + dt_i],
                             deps=[g_sw.get("s3")] if dt_i == 0 else ())
            for tt in range(8):
                ts0 = tt * 256
                for mt in range(DT):
                    pmA = p_mm[0 + mt % 2]
                    pmB = p_mm[2 + mt % 2]
                    for dt_i in range(DT):
                        ta = mm(pmA[:, :256],
                                sw1b[:, dt_i, mt * 128:mt * 128 + 128],
                                xn[:, dt_i, ts0:ts0 + 256],
                                dt_i == 0, dt_i == DT - 1,
                                deps=[tok_s1, t_xn[ts0 // 512],
                                      g_sw.get(("bank", mt % 2))]
                                if dt_i == 0 else (),
                                inc=(dt_i == DT - 1))
                    for dt_i in range(DT):
                        tb2 = mm(pmB[:, :256],
                                 sw2b[:, dt_i, mt * 128:mt * 128 + 128],
                                 xn[:, dt_i, ts0:ts0 + 256],
                                 dt_i == 0, dt_i == DT - 1,
                                 deps=[tok_s2, g_sw.get(("bank", 2 + mt % 2))]
                                 if dt_i == 0 else (),
                                 inc=(dt_i == DT - 1))
                    th1 = I("se", "activation", h1b[:, mt, :], pmA[:, :256],
                            AF.Silu, deps=[ta, g_sw.get(("hb", mt))],
                            inc=True)
                    g_sw[("bank", mt % 2)] = th1
                    tg = I("ve", "tensor_tensor", gb[:, mt, :], h1b[:, mt, :],
                           pmB[:, :256], ALU.mult,
                           deps=[th1, tb2, g_sw.get("gbg")], inc=True)
                    g_sw[("bank", 2 + mt % 2)] = tg
                    g_sw[("hb", mt)] = tg
                    t_gb = tg
                for dout in range(DT):
                    pm = p_mm[dout % 4]
                    for k in range(DT):
                        tko = mm(pm[:, :256],
                                 sw3b[:, k, dout * 128:dout * 128 + 128],
                                 gb[:, k, :], k == 0, k == DT - 1,
                                 deps=[t_gb, tok_s3, g_sw.get(("bank", dout % 4))]
                                 if k == 0 else (),
                                 inc=(k == DT - 1))
                    g_sw["gbg"] = tko
                    t_sh = I("ve", "scalar_tensor_tensor",
                             x[:, dout, ts0:ts0 + 256], pm[:, :256], 0.5,
                             x[:, dout, ts0:ts0 + 256], ALU.mult, ALU.add,
                             deps=[tko], inc=True)
                    g_sw[("bank", dout % 4)] = t_sh
            g_sw["s1"] = tko
            g_sw["s2"] = tko
            g_sw["s3"] = tko
        scope_close()
        return sch.now("ve")

    def attn(layer):
        B0 = bar()
        nh = 16 if layer == 0 else 8
        nhp = nh // 2
        GH = 4
        ngrp = nhp // GH
        wqkv = d_wqkv0 if layer == 0 else d_wqkv1
        d_wo = d_wo0 if layer == 0 else d_wo1
        nct = DT if layer == 0 else 4
        EPSD = 0.0 if layer == 0 else 1e-10

        sca = scope_open()
        gbe = None
        t_gbe = None
        if layer == 1:
            gbe = salloc(sca, "gbe", [128, 16], F32)
            t_gbe = dma("sq", gbe[:, 0:8], d_g1b1.ap()[0], deps=B0)
            t_gbe = dma("sq", gbe[:, 8:16], d_g1b1.ap()[1], deps=B0)
        t_xn = [None] * 4
        for w in range(4):
            st = ln_stats(w, deps=B0 + [T_X])
            tb0, tb1 = bcast2(st)
            last = None
            for dt_i in range(DT):
                xs = x[:, dt_i, 512 * w:512 * w + 512]
                I("ve", "tensor_tensor", lnpx[:, :], xs, p_bc[0][:, :],
                  ALU.subtract, deps=[tb0])
                last = I("ve", "tensor_tensor", xn[:, dt_i, 512 * w:512 * w + 512],
                         lnpx[:, :], p_bc[1][:, :], ALU.mult, deps=[tb1],
                         inc=True)
            G["pbc"] = last
            if layer == 1:
                if DBG == "a1pre":
                    tpd = None
                    for dt_i in range(DT):
                        tpd = dma("gp", dbg2.ap()[dt_i][:, 512 * w:512 * w + 512],
                                  xn[:, dt_i, 512 * w:512 * w + 512],
                                  deps=[last, tpd])
                    last = tpd
                st2 = ln_stats(w, deps=[last],
                               src_bf=lambda dt_i: xn[:, dt_i, 512 * w:512 * w + 512])
                tb0, tb1 = bcast2(st2)
                for dt_i in range(DT):
                    xsl = xn[:, dt_i, 512 * w:512 * w + 512]
                    I("ve", "tensor_tensor", lnpx[:, :], xsl, p_bc[0][:, :],
                      ALU.subtract, deps=[tb0, last])
                    tq2 = I("ve", "tensor_tensor", lnpx[:, :], lnpx[:, :],
                            p_bc[1][:, :], ALU.mult, deps=[tb1], inc=True)
                    last = I("se", "activation", xsl, lnpx[:, :], AF.Identity,
                             scale=gbe[:, dt_i:dt_i + 1],
                             bias=gbe[:, 8 + dt_i:9 + dt_i],
                             deps=[tq2, t_gbe], inc=True)
                G["pbc"] = last
            t_xn[w] = last
        t_xn_all = [t for t in t_xn]
        if DBG in ("a1", "a1pre"):
            for dt_i in range(DT):
                tdx = dma("sq", dbg3.ap()[dt_i], x[:, dt_i, :],
                          deps=B0 + t_xn)
                G["dbgx"] = tdx
                tdx = dma("gp", dbg2.ap()[dt_i], xn[:, dt_i, :],
                          deps=t_xn + [tdx])

        ctx = [salloc(sca, f"actx{i}", [128, S], BF16) for i in range(nhp)]
        mtmp = salloc(sca, "amtmp", [128, 8], F32)
        mfin = salloc(sca, "amfin", [128, 8], F32)
        mrec = rows[0:1, 0:512]
        ct64 = salloc(sca, "act64", [64, 512], BF16)
        bq8 = bk8 = bvr = bo8 = None
        if layer == 0:
            bq8 = salloc(sca, "abq", [128, 64], F32)
            bk8 = salloc(sca, "abk", [128, 64], F32)
            bvr = salloc(sca, "abvr", [1, 1024], BF16)
            t_bq = I("ve", "memset", bq8[:, :], 0.0, deps=B0, inc=True)
            t_bq = dma("sq", bq8[:, 0:8], d_bqk0.ap()[0], deps=B0 + [t_bq])
            t_bq = I("ve", "tensor_scalar", bq8[:, 0:64], bq8[:, 0:64], 0.125,
                     None, ALU.mult, deps=[t_bq], inc=True)
            t_bk = dma("sq", bk8[:, 0:8], d_bqk0.ap()[1], deps=B0)
            t_bv = dma("gp", bvr[:, :], d_bv0.ap(), deps=B0)
            bo8 = salloc(sca, "abo", [128, 8], F32)
            t_bo = dma("sq", bo8[:, :], d_bo0.ap(), deps=B0)
        else:
            t_bq = t_bk = t_bv = t_bo = None

        LIMIT = os.environ.get("KDEV_ALIM", "all")
        t_proj_done = []
        t_score_done = []
        G_ctx = {}

        scg = scope_open()
        Qg = [salloc(scg, f"aqg{i}", [128, S], BF16) for i in range(GH)]
        Kg = [salloc(scg, f"akg{i}", [128, S], BF16) for i in range(GH)]
        Vg = salloc(scg, "avg", [128, 16, 520], BF16)
        for g in range(ngrp):
            tbar_te = sch.now("te")
            scw = scope_open()
            wbuf = salloc(scw, "awbuf", [128, 8, 512], BF16)
            qoff = g * 512
            pgq = {}
            t_evac = {0: None, 1: None}

            def proj_qk2(sel, dest, scale, bias8):
                tok_w = None
                tprev = sch.now("te")
                for dt_i in range(DT):
                    src_ap = wqkv.ap()[sel][dt_i]
                    if layer == 0:
                        src_ap = src_ap[:, qoff:qoff + 512]
                    tok_w = dma("gp", wbuf[:, dt_i, :], src_ap,
                                deps=[t_evac[0], t_evac[1], tprev])
                tev = None
                for hl in range(GH):
                    hp = g * GH + hl
                    for tw in range(4):
                        pi = (hl * 4 + tw) % 2
                        pm = p_mm[pi]
                        dep0 = [tok_w, t_xn[tw], pgq.get(pi), tbar_te]
                        for dt_i in range(DT):
                            tokm = mm(pm[:, :],
                                      wbuf[:, dt_i, hl * 128:(hl + 1) * 128],
                                      xn[:, dt_i, tw * 512:tw * 512 + 512],
                                      dt_i == 0, dt_i == DT - 1,
                                      deps=dep0 if dt_i == 0 else (),
                                      inc=(dt_i == DT - 1))
                        kw = {}
                        if bias8 is not None:
                            kw["bias"] = bias8[:, hp:hp + 1]
                        tev = I("se", "activation",
                                dest[hl][:, tw * 512:tw * 512 + 512], pm[:, :],
                                AF.Identity, scale=scale,
                                deps=[tokm, t_bq, t_bk], inc=True, **kw)
                        pgq[pi] = tev
                        t_evac[pi] = tev
                return tev

            tQ = proj_qk2(0, Qg, 0.125, bq8)
            tK = proj_qk2(1, Kg, 1.0, bk8)
            tvm = I("ve", "memset", Vg[:, :, :], 1.0,
                    deps=B0 + [tbar_te], inc=True)
            tok_w = None
            tprev = sch.now("te")
            for dt_i in range(DT):
                src_ap = wqkv.ap()[2][dt_i]
                if layer == 0:
                    src_ap = src_ap[:, qoff:qoff + 512]
                tok_w = dma("gp", wbuf[:, dt_i, :], src_ap,
                            deps=[t_evac[0], t_evac[1], tprev])
            t_vev = {0: None, 1: None}
            for tc in range(16):
                pi = tc % 2
                pm = p_mm[pi]
                dep0 = [tok_w, t_xn[tc // 4], t_vev[pi]]
                for dt_i in range(DT):
                    tokm = mm(pm[:, :], xn[:, dt_i, tc * 128:tc * 128 + 128],
                              wbuf[:, dt_i, :], dt_i == 0,
                              (dt_i == DT - 1) and layer != 0,
                              deps=dep0 if dt_i == 0 else (),
                              inc=(dt_i == DT - 1) and layer != 0)
                if layer == 0:
                    tokm = mm(pm[:, :], ones[0:1, 0:128],
                              bvr[0:1, qoff:qoff + 512],
                              False, True, deps=[t_bv], inc=True)
                tev = None
                for hl8 in range(8):
                    tev = I("ve", "tensor_copy",
                            Vg[:, tc, hl8 * 65:hl8 * 65 + 64],
                            pm[:, hl8 * 64:hl8 * 64 + 64],
                            deps=[tokm, tvm], inc=True)
                t_vev[pi] = tev
            t_proj_done.append(tev)

            scope_close()
            if LIMIT == "proj":
                continue

            scb = scope_open()
            Kxt = salloc(scb, "akxt", [96, S], BF16)
            Qxt = salloc(scb, "aqxt", [96, S], BF16)
            ptst = [salloc(scb, f"aptst{i}", [128, 512], BF16) for i in range(2)]
            mall = salloc(scb, "amall", [128, 32], F32)
            mall32 = salloc(scb, "amall32", [128, 128], F32)
            t_m32z = I("ve", "memset", mall32[:, :], 0.0, deps=B0, inc=True)
            maskB = salloc(scb, "amaskB", [128, 896], BF16)
            t_mb = dma("gp", maskB[:, :], (d_maskB0 if layer == 0 else d_maskB1).ap(),
                       deps=B0)
            maskA = None
            t_ma = None
            if layer == 1:
                maskA = salloc(scb, "amaskA", [128, 896], BF16)
                t_ma = dma("gp", maskA[:, :], d_maskA1.ap(), deps=B0)

            G_k = {}
            vbar = sch.now("ve")
            G_k[("A", 0)] = vbar
            G_k[("A", 1)] = vbar
            t_exp = {0: vbar, 1: vbar}
            t_pv = {0: None, 1: None}
            t_pairdone = None
            t_headdone = None
            NHL = int(os.environ.get("KDEV_NHL", str(GH)))
            NSP = int(os.environ.get("KDEV_NSP", "4"))
            for hl in range(min(GH, NHL) if LIMIT != "setup" else 0):
                hp = g * GH + hl
                for s in range(NSP):
                    for qtl in range(4):
                        qt = s * 4 + qtl
                        nch = s + 1
                        for c in range(nch):
                            par = c % 2
                            pA = p_mm[0 + 2 * par]
                            pB = p_mm[1 + 2 * par]
                            depq = [tQ, tK, G_k.get(("A", par)),
                                    t_exp[0], t_exp[1]]
                            for (rg, cg) in ((0, 0), (0, 64), (64, 0), (64, 64)):
                                po = pA if rg == 0 else pB
                                tokq = mm(po[cg:cg + 64, :],
                                          Qg[hl][rg:rg + 64,
                                                 qt * 128 + cg:qt * 128 + cg + 64],
                                          Kg[hl][rg:rg + 64, c * 512:c * 512 + 512],
                                          True, True,
                                          deps=depq if (rg == 0 and cg == 0) else (),
                                          inc=(rg == 64 and cg == 64),
                                          tile_position=(rg, cg))
                            if LIMIT == "pq":
                                continue
                            if layer == 1 and c == s:
                                u = qt % 4
                                msl = maskA[:, 384 - 128 * u:896 - 128 * u]
                                I("ve", "tensor_tensor", pA[:, :], pA[:, :],
                                  msl, ALU.add, deps=[tokq, t_ma])
                                tokq = I("ve", "tensor_tensor", pB[:, :],
                                         pB[:, :], msl, ALU.add, inc=True)
                            wext = qt * 128 + 128 - c * 512
                            apA = pA[:, 0:wext] if wext < 512 else pA[:, :]
                            apB = pB[:, 0:wext] if wext < 512 else pB[:, :]
                            ta = I("ve", "tensor_reduce", mtmp[:, c:c + 1],
                                   apA, AX.X, ALU.max, deps=[tokq], inc=True)
                            tb = I("ve", "tensor_reduce", mtmp[:, 4 + c:5 + c],
                                   apB, AX.X, ALU.max, deps=[tokq], inc=True)
                            G_k[("A", par)] = tb
                        if LIMIT == "pq":
                            continue
                        I("ve", "tensor_reduce", mall[:, qt:qt + 1],
                          mtmp[:, 0:nch], AX.X, ALU.max, negate=True,
                          deps=[G_k.get("mfin")], inc=True)
                        G_k["lastred"] = I(
                            "ve", "tensor_reduce", mall[:, 16 + qt:17 + qt],
                            mtmp[:, 4:4 + nch], AX.X, ALU.max, negate=True,
                            inc=True)
                t_A_done = G_k["lastred"]
                if LIMIT in ("pa", "pq"):
                    continue
                for hsel in range(2):
                    h_local8 = hl * 2 + hsel
                    db = [t_headdone, tK, tQ]
                    t_z = I("ve", "memset", Kxt[64:96, :], 0.0, deps=db,
                            inc=True)
                    t_z = I("ve", "memset", Kxt[64:65, :], 1.0, inc=True)
                    t_z = I("ve", "memset", Qxt[64:96, :], 0.0, inc=True)
                    tkx = I("ve", "tensor_copy", Kxt[0:64, :],
                            Kg[hl][hsel * 64:hsel * 64 + 64, :], deps=db,
                            inc=True)
                    tqx = I("ve", "tensor_copy", Qxt[0:64, :],
                            Qg[hl][hsel * 64:hsel * 64 + 64, :], deps=db,
                            inc=True)
                    pctx = p_bc[0][0:65, :]
                    for s in range(4):
                        qs = s * 512
                        tsp = None
                        for u in range(4):
                            qt = s * 4 + u
                            tsp = I("ve", "tensor_reduce",
                                    mall32[:, u * 32:u * 32 + 1],
                                    mall[:, hsel * 16 + qt:hsel * 16 + qt + 1],
                                    AX.X, ALU.max,
                                    deps=[t_A_done, G_k.get("m32"), t_m32z],
                                    inc=True)
                        ttr = I("te", "transpose", p_st[0][:, 0:128],
                                mall32[:, :], ident[:, :],
                                deps=[tsp, T_ID, G_k.get("tr")], inc=True)
                        G_k["m32"] = ttr
                        tf = None
                        for u in range(4):
                            tf = I("ve", "tensor_copy",
                                   Qxt[64:65, qs + u * 128:qs + (u + 1) * 128],
                                   p_st[0][u * 32:u * 32 + 1, 0:128],
                                   deps=[ttr, tqx, G_k.get("qrow")], inc=True)
                        G_k["tr"] = tf
                        G_k["mfin"] = ttr
                        if LIMIT == "pfill":
                            continue
                        nkc = 4 * (s + 1)
                        for kc in range(nkc):
                            par = kc % 2
                            pm = p_mm[2 + par]
                            depb = [tf, tkx, t_z, G_k.get(("B", par)),
                                    t_exp[par]]
                            tokb = mm(pm[:, :],
                                      Kxt[0:96, kc * 128:kc * 128 + 128],
                                      Qxt[0:96, qs:qs + 512],
                                      True, True, deps=depb, inc=True)
                            if kc >= 4 * s:
                                u = kc - 4 * s
                                tokb = I("ve", "tensor_tensor", pm[:, :],
                                         pm[:, :],
                                         maskB[:, 384 - 128 * u:896 - 128 * u],
                                         ALU.add, deps=[tokb, t_mb], inc=True)
                            texp = I("se", "activation", ptst[par][:, :],
                                     pm[:, :], AF.Exp,
                                     deps=[tokb, t_pv[par]], inc=True)
                            t_exp[par] = texp
                            G_k[("B", par)] = texp
                            tpv = mm(pctx, Vg[:, kc, h_local8 * 65:
                                              h_local8 * 65 + 65],
                                     ptst[par][:, :], kc == 0, kc == nkc - 1,
                                     deps=[texp, t_proj_done[-1],
                                           G_ctx.get(0), G["pbc"]]
                                     if kc == 0 else [texp], inc=True)
                            t_pv[par] = tpv
                            t_pairdone = tokb
                            t_headdone = tokb
                        if LIMIT == "pb":
                            continue
                        if EPSD != 0.0:
                            trc = I("ve", "tensor_scalar", mrec[0:1, :],
                                    pctx[64:65, :], EPSD, None, ALU.add,
                                    deps=[tpv], inc=True)
                            trc = I("ve", "reciprocal", mrec[0:1, :],
                                    mrec[0:1, :], inc=True)
                        else:
                            trc = I("ve", "reciprocal", mrec[0:1, :],
                                    pctx[64:65, :], deps=[tpv], inc=True)
                        tev = I("se", "activation", ct64[:, :], pctx[0:64, :],
                                AF.Copy, deps=[tpv, G_k.get("rb")], inc=True)
                        tbc = mm(p_st[1][0:64, :], ones_f[0:1, 0:64],
                                 mrec[0:1, :], True, True,
                                 deps=[trc, G_k.get("rb")], inc=True)
                        tw2 = I("ve", "tensor_tensor",
                                ctx[hp][hsel * 64:hsel * 64 + 64, qs:qs + 512],
                                ct64[:, :], p_st[1][0:64, :], ALU.mult,
                                deps=[tbc, tev], inc=True)
                        G_k["rb"] = tw2
                        G_ctx[0] = tw2
                        G_k["qrow"] = tokb
                        t_score_done.append(tw2)
            scope_close()

        scope_close()
        if LIMIT in ("proj", "scores"):
            scope_close()
            return sch.now("ve")
        scw2 = scope_open()
        wob = salloc(scw2, "awob", [128, nct, 1024], BF16)
        t_wo = None
        bwo = bar()
        for ct_i in range(nct):
            t_wo = dma("gp", wob[:, ct_i, :], d_wo.ap()[ct_i],
                       deps=bwo if ct_i == 0 else ())
        t_badd = None
        if layer == 0:
            for dout in range(DT):
                t_badd = I("ve", "tensor_scalar", x[:, dout, :], x[:, dout, :],
                           bo8[:, dout:dout + 1], None, ALU.add,
                           deps=[t_bo] + t_score_done[-2:], inc=True)
        dep_sc = t_score_done[-2:]
        pgo = {}
        for dout in range(DT):
            for tw in range(4):
                pi = (dout * 4 + tw) % 4
                pm = p_mm[pi]
                for ct_i in range(nct):
                    toko = mm(pm[:, :], wob[:, ct_i, dout * 128:(dout + 1) * 128],
                              ctx[ct_i][:, tw * 512:tw * 512 + 512],
                              ct_i == 0, ct_i == nct - 1,
                              deps=([t_wo, pgo.get(pi)] + dep_sc)
                              if ct_i == 0 else (),
                              inc=(ct_i == nct - 1))
                tevo = I("ve", "scalar_tensor_tensor",
                         x[:, dout, tw * 512:tw * 512 + 512], pm[:, :], 1.0,
                         x[:, dout, tw * 512:tw * 512 + 512], ALU.mult,
                         ALU.add, deps=[toko, t_badd, G.get("dbgx")],
                         inc=True)
                pgo[pi] = tevo
        scope_close()
        scope_close()
        return sch.now("ve")

    l0ffn(sc0)
    scope_close()
    if n_sub >= 2:
        attn(0)
    if n_sub >= 3:
        sc2 = scope_open()
        alloc_arena(sc2)
        l0ffn(sc2)
        scope_close()
    if n_sub >= 4:
        sc3 = scope_open()
        moe(sc3)
        scope_close()
    if n_sub >= 5:
        attn(1)
    if n_sub >= 6:
        sc5 = scope_open()
        moe(sc5)
        scope_close()
    t_done = sch.now("ve")

    for dt_i in range(DT):
        dma("sq", x_out.ap()[dt_i], x[:, dt_i, :], deps=[t_done])

    def fin(e, n=sch.cnt["sq"], sem=sch.sem["sq"]):
        e.wait_ge(sem, n)
    sch.ops["sq"].append(fin)

    sch.finalize()
    return nc


def stage_weights(inputs):
    w1 = np.asarray(inputs["L0_ffn_w1"], dtype=np.float32)
    w2 = np.asarray(inputs["L0_ffn_w2"], dtype=np.float32)
    t1 = np.ascontiguousarray(
        w1.T.reshape(DT, 128, 2, 2048).transpose(0, 2, 1, 3).reshape(16, 128, 2048))
    t2 = np.ascontiguousarray(
        w2.T.reshape(32, 128, 1024).reshape(16, 2, 128, 1024)
        .transpose(0, 2, 1, 3).reshape(16, 128, 2048))
    b1 = np.ascontiguousarray(
        np.asarray(inputs["L0_ffn_b1"], np.float32).reshape(32, 128).T)
    b2 = np.ascontiguousarray(
        np.asarray(inputs["L0_ffn_b2"], np.float32).reshape(8, 128).T)
    out = {"d_w1": t1, "d_w2": t2, "d_b1": b1, "d_b2": b2}

    Wq = np.asarray(inputs["L0_attn_Wq"], np.float32)
    Wk = np.asarray(inputs["L0_attn_Wk"], np.float32)
    Wv = np.asarray(inputs["L0_attn_Wv"], np.float32)
    qkv0 = np.stack([w.transpose(1, 0, 2).reshape(D, 1024).reshape(DT, 128, 1024)
                     for w in (Wq, Wk, Wv)], 0)
    out["d_wqkv0"] = np.ascontiguousarray(qkv0)
    bq = np.asarray(inputs["L0_attn_bq"], np.float32).reshape(8, 128).T
    bk = np.asarray(inputs["L0_attn_bk"], np.float32).reshape(8, 128).T
    out["d_bqk0"] = np.ascontiguousarray(np.stack([bq, bk], 0))
    out["d_bv0"] = np.ascontiguousarray(
        np.asarray(inputs["L0_attn_bv"], np.float32).reshape(1, 1024))
    out["d_wo0"] = np.ascontiguousarray(
        np.asarray(inputs["L0_attn_Wo"], np.float32).reshape(DT, 128, 1024))
    out["d_bo0"] = np.ascontiguousarray(
        np.asarray(inputs["L0_attn_bo"], np.float32).reshape(8, 128).T)
    jj = np.arange(896)[None, :]
    pp = np.arange(128)[:, None]
    out["d_maskB0"] = np.where(jj < pp + 384, np.float32(-1e9),
                               np.float32(0)).astype(np.float32)
    wq1 = np.asarray(inputs["L1_attn_w_qkv"], np.float32)
    qkv1 = np.stack([wq1[:, s * 512:(s + 1) * 512].reshape(DT, 128, 512)
                     for s in range(3)], 0)
    out["d_wqkv1"] = np.ascontiguousarray(qkv1)
    out["d_wo1"] = np.ascontiguousarray(
        np.asarray(inputs["L1_attn_w_out"], np.float32).reshape(4, 128, 1024))
    g1 = np.asarray(inputs["L1_attn_ln_gamma"], np.float32).reshape(8, 128).T
    be1 = np.asarray(inputs["L1_attn_ln_beta"], np.float32).reshape(8, 128).T
    out["d_g1b1"] = np.ascontiguousarray(np.stack([g1, be1], 0))
    out["d_maskA1"] = np.where(jj > pp + 384, np.float32(-1e38),
                               np.float32(0)).astype(np.float32)
    out["d_maskB1"] = np.where(jj < pp + 384, np.float32(-1e38),
                               np.float32(0)).astype(np.float32)
    gw = np.asarray(inputs["L1_ffn_gate_w"], np.float32)
    out["d_gw"] = np.ascontiguousarray(gw.T.reshape(DT, 128, 8))
    for nm, key in (("d_we1", "L1_ffn_w1"), ("d_we2", "L1_ffn_w2"),
                    ("d_we3", "L1_ffn_w3")):
        wv = np.asarray(inputs[key], np.float32)
        out[nm] = np.ascontiguousarray(
            wv.transpose(0, 2, 1).reshape(8, DT, 128, 1024))
    for nm, key in (("d_be1", "L1_ffn_b1"), ("d_be2", "L1_ffn_b2"),
                    ("d_be3", "L1_ffn_b3")):
        bv = np.asarray(inputs[key], np.float32)
        out[nm] = np.ascontiguousarray(
            bv.reshape(8, 8, 128).transpose(0, 2, 1))
    out["d_sw1"] = np.ascontiguousarray(
        np.asarray(inputs["L1_ffn_sw1"], np.float32).T.reshape(DT, 128, 2048))
    out["d_sw2"] = np.ascontiguousarray(
        np.asarray(inputs["L1_ffn_sw2"], np.float32).T.reshape(DT, 128, 2048))
    out["d_sw3"] = np.ascontiguousarray(
        np.asarray(inputs["L1_ffn_sw3"], np.float32).T.reshape(16, 128, 1024))
    return out


def kernel(**inputs):
    nc = build_program(N_SUB)
    shared = stage_weights(inputs)
    xfull = np.asarray(inputs["L0_attn_embeddings"], np.float32)
    in_maps = []
    for c in range(NCORE):
        xfm = np.ascontiguousarray(xfull[c].T.reshape(DT, 128, S))
        im = {"x_in": xfm}
        im.update(shared)
        in_maps.append(im)
    res = run_bass_kernel_spmd(nc, in_maps, core_ids=list(range(NCORE)))
    outs = []
    for c in range(NCORE):
        o = np.asarray(res.results[c]["x_out"])
        outs.append(o.transpose(2, 0, 1).reshape(S, D))
    return np.stack(outs, 0)


def kernel_debug(inputs):
    nc = build_program(N_SUB)
    shared = stage_weights(inputs)
    xfull = np.asarray(inputs["L0_attn_embeddings"], np.float32)
    im = {"x_in": np.ascontiguousarray(xfull[0].T.reshape(DT, 128, S))}
    im.update(shared)
    res = run_bass_kernel_spmd(nc, [im], core_ids=[0])
    return {k: np.asarray(v) for k, v in res.results[0].items()}


# version 02
# speedup vs baseline: 1.0010x; 1.0010x over previous
import os
import numpy as np
import ml_dtypes
import concourse.bass as bass
import concourse.mybir as mybir
from concourse.bass_utils import run_bass_kernel_spmd
from contextlib import ExitStack

F32 = mybir.dt.float32
BF16 = mybir.dt.bfloat16
I32 = mybir.dt.int32
AF = mybir.ActivationFunctionType
ALU = mybir.AluOpType
AX = mybir.AxisListType
BF = ml_dtypes.bfloat16

S = 2048
D = 1024
DT = 8
NCORE = 8

N_SUB = int(os.environ.get("KDEV_NSUB", "6"))


class Sched:

    def __init__(self, nc, es):
        self.nc = nc
        self.ops = {e: [] for e in ("te", "ve", "se", "gp", "sq")}
        self.cnt = {s: 0 for s in ("te", "ve", "se", "gp", "gq", "sq")}
        self.sem = {s: es.enter_context(nc.semaphore("sem_" + s))
                    for s in self.cnt}
        self.wm = {e: {} for e in self.ops}
        self.last_gp = None

    def _resolve(self, eng, deps, same_ok=True):
        out = []
        for d in deps:
            if d is None:
                continue
            sname, val = d
            if sname == eng and same_ok:
                continue
            w = self.wm[eng].get(sname, 0)
            if val > w:
                out.append((self.sem[sname], val))
                self.wm[eng][sname] = val
        return out

    def I(self, eng, fn, *args, deps=(), inc=False, **kw):
        if eng == "gp":
            inc = True
            deps = list(deps) + [self.last_gp]
        waits = self._resolve(eng, deps, same_ok=(eng != "gp"))
        tok = None
        if inc:
            self.cnt[eng] += 1
            tok = (eng, self.cnt[eng])
        sem = self.sem[eng]

        def thunk(e, waits=waits, fn=fn, args=args, kw=kw, inc=inc, sem=sem):
            for s, v in waits:
                e.wait_ge(s, v)
            ins = getattr(e, fn)(*args, **kw)
            if inc:
                ins.then_inc(sem, 1)
        self.ops[eng].append(thunk)
        if eng == "gp":
            self.last_gp = tok
        return tok

    def dma(self, queue, dst, src, deps=(), **kw):
        sname = "sq" if queue == "sq" else "gq"
        waits = self._resolve(queue, deps)
        self.cnt[sname] += 16
        tok = (sname, self.cnt[sname])
        sem = self.sem[sname]

        def thunk(e, waits=waits, dst=dst, src=src, kw=kw, sem=sem):
            for s, v in waits:
                e.wait_ge(s, v)
            e.dma_start(dst, src, **kw).then_inc(sem, 16)
        self.ops[queue].append(thunk)
        return tok

    def mm(self, out, lhsT, rhs, start, stop, deps=(), inc=False,
           tile_position=None):
        kw = {}
        if tile_position is not None:
            kw["tile_position"] = tile_position
        return self.I("te", "matmul", out, lhsT, rhs, start=start, stop=stop,
                      deps=deps, inc=inc, **kw)

    def now(self, eng):
        return (eng, self.cnt[eng])

    def finalize(self):
        nc = self.nc
        with nc.Block() as blk:
            @blk.tensor
            def _(e):
                for f in self.ops["te"]:
                    f(e)

            @blk.vector
            def _(e):
                for f in self.ops["ve"]:
                    f(e)

            @blk.scalar
            def _(e):
                for f in self.ops["se"]:
                    f(e)

            @blk.gpsimd
            def _(e):
                for f in self.ops["gp"]:
                    f(e)

            @blk.sync
            def _(e):
                for f in self.ops["sq"]:
                    f(e)


def build_program(n_sub):
    nc = bass.Bass("TRN2", target_bir_lowering=False,
                   detect_race_conditions=False)
    es = ExitStack()
    sch = Sched(nc, es)
    I, dma, mm = sch.I, sch.dma, sch.mm

    def dram(name, shape, dt=F32, kind="ExternalInput"):
        return nc.dram_tensor(name, shape, dt, kind=kind)

    sbuf = nc.alloc_sbuf_tensor
    psum_a = nc.alloc_psum_tensor

    x_in = dram("x_in", [DT, 128, S])
    x_out = dram("x_out", [DT, 128, S], kind="ExternalOutput")
    d_w1 = dram("d_w1", [16, 128, 2048])
    d_w2 = dram("d_w2", [16, 128, 2048])
    d_b1 = dram("d_b1", [128, 32])
    d_b2 = dram("d_b2", [128, 8])
    d_wqkv0 = dram("d_wqkv0", [3, 8, 128, 1024])
    d_bqk0 = dram("d_bqk0", [2, 128, 8])
    d_bv0 = dram("d_bv0", [1, 1024])
    d_wo0 = dram("d_wo0", [8, 128, 1024])
    d_bo0 = dram("d_bo0", [128, 8])
    d_maskB0 = dram("d_maskB0", [128, 896])
    d_wqkv1 = dram("d_wqkv1", [3, 8, 128, 512])
    d_wo1 = dram("d_wo1", [4, 128, 1024])
    d_g1b1 = dram("d_g1b1", [2, 128, 8])
    d_maskA1 = dram("d_maskA1", [128, 896])
    d_maskB1 = dram("d_maskB1", [128, 896])
    d_gw = dram("d_gw", [DT, 128, 8])
    d_we1 = dram("d_we1", [8, DT, 128, 1024])
    d_we2 = dram("d_we2", [8, DT, 128, 1024])
    d_we3 = dram("d_we3", [8, DT, 128, 1024])
    d_be1 = dram("d_be1", [8, 128, 8])
    d_be2 = dram("d_be2", [8, 128, 8])
    d_be3 = dram("d_be3", [8, 128, 8])
    d_sw1 = dram("d_sw1", [DT, 128, 2048])
    d_sw2 = dram("d_sw2", [DT, 128, 2048])
    d_sw3 = dram("d_sw3", [16, 128, 1024])
    s_w1q = dram("s_w1q", [16, 128, 2048], BF16, kind="Internal")
    s_w2q = dram("s_w2q", [16, 128, 2048], BF16, kind="Internal")
    dbg_out = dram("dbg_out", [DT, 128, S], F32, kind="ExternalOutput")
    dbg2 = dram("dbg2", [DT, 128, S], F32, kind="ExternalOutput")
    dbg3 = dram("dbg3", [DT, 128, S], F32, kind="ExternalOutput")
    DBG = os.environ.get("KDEV_DEBUG", "")

    x = sbuf("x", [128, DT, S], F32)
    xn = sbuf("xn", [128, DT, S], BF16)
    ones = sbuf("ones", [128, 128], BF16)
    ones_f = sbuf("ones_f", [1, 128], F32)
    rows = sbuf("rows", [1, 3 * 512], F32)
    ident = sbuf("ident", [128, 128], F32)
    consts = sbuf("consts", [1, 8], F32)
    b1q = sbuf("b1q", [128, 64], F32)
    b2q = sbuf("b2q", [128, 64], F32)
    lnc = sbuf("lnc", [128, 1, 512], BF16)
    lnsq = sbuf("lnsq", [128, 1, 512], BF16)
    lnlo = sbuf("lnlo", [128, 1, 512], BF16)
    lnpx = sbuf("lnpx", [128, 512], F32)

    p_mm = [psum_a(f"p_mm{i}", [128, 512], F32) for i in range(4)]
    p_st = [psum_a(f"p_st{i}", [128, 512], F32) for i in range(2)]
    p_bc = [psum_a(f"p_bc{i}", [128, 512], F32) for i in range(2)]

    for dt_i in range(DT):
        T_X = dma("sq", x[:, dt_i, :], x_in.ap()[dt_i])
    I("gp", "memset", ones[:, :], 1.0, inc=True)
    I("gp", "memset", ones_f[:, :], 1.0, inc=True)
    I("gp", "memset", ident[:, :], 1.0, inc=True)
    T_ID = I("gp", "affine_select", out=ident[:, :], in_=ident[:, :],
             compare_op=ALU.is_equal, fill=0.0, base=0,
             pattern=[[-1, 128]], channel_multiplier=1, inc=True)
    T_CONST = I("gp", "memset", consts[0:1, 0:1], 1e-5, inc=True)

    G = {"pst": None, "pbc": None, "q": None}

    def bar():
        return [sch.now(e) for e in ("te", "ve", "se", "gp", "gq", "sq")]

    scope_stack = []

    def scope_open():
        scope_stack.append(ExitStack())
        return scope_stack[-1]

    _uniq = [0]

    def salloc(scope, name, shape, dt):
        _uniq[0] += 1
        return scope.enter_context(
            nc.sbuf_tensor(f"{name}_{_uniq[0]}", shape, dt))

    def scope_close():
        scope_stack.pop().close()

    A = {}

    def quant_rne(fsrc_ap, out_ap, n, scale_in, scale_out, deps):
        z, f, c = A["qa"][:, :n], A["qb"][:, :n], A["qc"][:, :n]
        ii = A["qi"][:, :n]
        d0 = list(deps) + ([G["q"]] if G["q"] else [])
        I("ve", "tensor_scalar", z, fsrc_ap, float(scale_in), None, ALU.mult,
          deps=d0)
        I("ve", "tensor_copy", ii, z)
        I("ve", "tensor_copy", f, ii)
        I("ve", "tensor_tensor", c, f, z, ALU.subtract)
        I("ve", "tensor_scalar", z, c, 0.5, None, ALU.is_gt)
        I("ve", "tensor_tensor", f, f, z, ALU.subtract)
        I("ve", "tensor_scalar", z, c, -0.5, None, ALU.is_lt)
        I("ve", "tensor_tensor", f, f, z, ALU.add)
        I("ve", "tensor_scalar", f, f, -128.0, 127.0, ALU.max, ALU.min)
        tok = I("ve", "tensor_scalar", out_ap, f, float(scale_out), None,
                ALU.mult, inc=True)
        G["q"] = tok
        return tok

    def ln_stats(w, deps, rscale=1.0, src_bf=None, slot=0, hilo=False):
        t0 = 512 * w
        m = rows[:, 0:512]
        v = rows[:, 512:1024]
        r = rows[:, 1024:1536]
        tok_s = None
        tok_m = None
        prev_mm = {}
        for dt_i in range(DT):
            first, last = dt_i == 0, dt_i == DT - 1
            sl = dt_i % 2
            xs = x[:, dt_i, t0:t0 + 512]
            if src_bf is None:
                tc = I("ve", "tensor_copy", lnc[:, 0, :], xs,
                       deps=list(deps) + [prev_mm.get(("m", 0)),
                                          prev_mm.get(("sq", 0))], inc=True)
                chunk = lnc[:, 0, :]
                dchunk = [tc]
            else:
                chunk = src_bf(dt_i)
                dchunk = list(deps)
            pdep = [G["pst"]] if first else []
            if hilo:
                tlo = I("ve", "tensor_tensor", lnlo[:, 0, :], xs, chunk,
                        ALU.subtract, deps=dchunk + [prev_mm.get(("lo", 0))],
                        inc=True)
                tpx = I("ve", "tensor_tensor", lnpx[:, :], xs, xs, ALU.mult,
                        inc=True)
                th2 = I("ve", "tensor_copy", lnsq[:, 0, :], lnpx[:, :],
                        deps=[prev_mm.get(("sq", 0))], inc=True)
                tm1 = mm(p_st[0][0:1, :], ones[:, 0:1], chunk, first, False,
                         deps=dchunk + pdep, inc=True)
                tm2 = mm(p_st[0][0:1, :], ones[:, 0:1], lnlo[:, 0, :], False,
                         last, deps=[tlo], inc=True)
                tq1 = mm(p_st[1][0:1, :], ones[:, 0:1], lnsq[:, 0, :], first,
                         False, deps=[th2] + pdep, inc=True)
                tl2 = I("ve", "tensor_tensor", lnlo[:, 0, :], lnpx[:, :],
                        lnsq[:, 0, :], ALU.subtract, deps=[tm2], inc=True)
                tok_s = mm(p_st[1][0:1, :], ones[:, 0:1], lnlo[:, 0, :], False,
                           last, deps=[tl2], inc=True)
                prev_mm[("m", 0)] = tm1
                prev_mm[("sq", 0)] = tq1
                prev_mm[("lo", 0)] = tok_s
            else:
                tq = I("se", "activation", lnsq[:, 0, :], chunk, AF.Square,
                       deps=dchunk + [prev_mm.get(("sq", 0))], inc=True)
                tm1 = mm(p_st[0][0:1, :], ones[:, 0:1], chunk, first, last,
                         deps=dchunk + pdep, inc=True)
                tok_s = mm(p_st[1][0:1, :], ones[:, 0:1], lnsq[:, 0, :], first,
                           last, deps=[tq], inc=True)
                prev_mm[("m", 0)] = tm1
                prev_mm[("sq", 0)] = tok_s
        t1 = I("ve", "tensor_scalar", m, p_st[0][0:1, :512], 1.0 / D, None,
               ALU.mult, deps=[tok_s, G.get("rows")], inc=True)
        t2 = I("ve", "tensor_scalar", v, p_st[1][0:1, :512], 1.0 / D, None,
               ALU.mult, deps=[tok_s], inc=True)
        G["pst"] = t2
        I("ve", "tensor_tensor", r, m, m, ALU.mult)
        tv = I("ve", "tensor_tensor", v, v, r, ALU.subtract, inc=True)
        tve = I("ve", "tensor_scalar", v, v, 1e-5, None, ALU.add, deps=[tv],
                inc=True)
        tl = I("se", "activation", r, v, AF.Ln, deps=[tve], inc=True)
        tr0 = I("se", "activation", r, r, AF.Exp, scale=-0.5, deps=[tl],
                inc=True)
        tmp = lnpx[0:1, :]
        I("ve", "tensor_tensor", tmp, r, r, ALU.mult, deps=[tr0])
        I("ve", "tensor_tensor", tmp, tmp, v, ALU.mult)
        I("ve", "tensor_scalar", tmp, tmp, -0.5, 1.5, ALU.mult, ALU.add)
        tr = I("ve", "tensor_tensor", r, r, tmp, ALU.mult, inc=True)
        if rscale != 1.0:
            trb = I("ve", "tensor_scalar", v, r, float(rscale), None,
                    ALU.mult, deps=[tr], inc=True)
            rsc = v
        else:
            trb = tr
            rsc = r
        return dict(m=m, v=v, r=r, rsc=rsc, t_m=t1, t_r=tr, t_rb=trb)

    def bcast2(st):
        dep_g = [G["pbc"]] if G["pbc"] else []
        tb0 = mm(p_bc[0][:, :], ones_f[0:1, :], st["m"], True, True,
                 deps=[st["t_m"]] + dep_g, inc=True)
        tb1 = mm(p_bc[1][:, :], ones_f[0:1, :], st["rsc"], True, True,
                 deps=[st["t_rb"]] + dep_g, inc=True)
        G["rows"] = tb1
        return tb0, tb1

    def alloc_arena(sc):
        for nm, dt_ in (("qa", F32), ("qb", F32), ("qc", F32)):
            A[nm] = salloc(sc, nm, [128, 2048], dt_)
        A["qi"] = salloc(sc, "qi", [128, 2048], I32)
        A["qj"] = salloc(sc, "qj", [128, 2048], I32)

    sc0 = scope_open()
    alloc_arena(sc0)
    wq_f = salloc(sc0, "wq_f", [128, 2048], F32)
    wq_b = salloc(sc0, "wq_b", [128, 2048], BF16)
    t_w1q, t_w2q = [], []
    t_chain = None
    t_store = None
    for j in range(32):
        srcj = (d_w1 if j < 16 else d_w2).ap()[j % 16]
        dst = (s_w1q if j < 16 else s_w2q).ap()[j % 16]
        tl = dma("sq", wq_f[:, :], srcj,
                 deps=[t_chain] if t_chain else [])
        tq = quant_rne(wq_f[:, :], wq_b[:, :], 2048, 64.0, 1.0 / 1024.0,
                       deps=[tl, t_store])
        t_chain = tq
        ts_ = dma("gp", dst, wq_b[:, :], deps=[tq])
        t_store = ts_
        (t_w1q if j < 16 else t_w2q).append(ts_)
    tz = I("ve", "memset", wq_f[:, 0:64], 0.0, deps=[t_chain], inc=True)
    t_bl = dma("sq", wq_f[:, 0:32], d_b1.ap(), deps=[tz])
    t_b1 = quant_rne(wq_f[:, 0:64], b1q[:, 0:64], 64, 64.0, 16.0,
                     deps=[t_bl])
    tz2 = I("ve", "memset", wq_f[:, 64:128], 0.0, deps=[t_b1], inc=True)
    t_bl2 = dma("sq", wq_f[:, 64:72], d_b2.ap(), deps=[tz2])
    t_b2 = quant_rne(wq_f[:, 64:128], b2q[:, 0:64], 64, 64.0, 1.0 / 128.0,
                     deps=[t_bl2])

    def l0ffn(sc):
        B0 = bar()
        w1buf = salloc(sc, "w1buf", [128, 8, 1024], BF16)
        w2buf = salloc(sc, "w2buf", [128, 8, 1024], BF16)
        hq = salloc(sc, "hq", [128, 2048], BF16)
        kf = salloc(sc, "kf", [128, 2048], F32)
        t_xq = [None] * 4
        for w in range(4):
            st = ln_stats(w, deps=B0 + [T_X], rscale=16.0, slot=w % 2,
                          hilo=True)
            tb0, tb1 = bcast2(st)
            last = None
            for dt_i in range(DT):
                xs = x[:, dt_i, 512 * w:512 * w + 512]
                za, f, c = A["qa"][:, :512], A["qb"][:, :512], A["qc"][:, :512]
                ii = A["qi"][:, :512]
                I("ve", "tensor_tensor", za, xs, p_bc[0][:, :], ALU.subtract,
                  deps=[tb0, T_X] + ([G["q"]] if G["q"] and dt_i == 0 else []))
                I("ve", "tensor_tensor", za, za, p_bc[1][:, :], ALU.mult,
                  deps=[tb1])
                I("ve", "tensor_copy", ii, za)
                I("ve", "tensor_copy", f, ii)
                I("ve", "tensor_tensor", c, f, za, ALU.subtract)
                I("ve", "tensor_scalar", za, c, 0.5, None, ALU.is_gt)
                I("ve", "tensor_tensor", f, f, za, ALU.subtract)
                I("ve", "tensor_scalar", za, c, -0.5, None, ALU.is_lt)
                I("ve", "tensor_tensor", f, f, za, ALU.add)
                I("ve", "tensor_scalar", f, f, -128.0, 127.0, ALU.max, ALU.min)
                last = I("ve", "tensor_copy",
                         xn[:, dt_i, 512 * w:512 * w + 512], f, inc=True)
            t_xq[w] = last
            G["q"] = last
            G["pbc"] = last
        if DBG == "w1q":
            for j in range(8):
                dma("gp", dbg_out.ap()[j], s_w1q.ap()[j][:, 0:2048],
                    deps=[t_w1q[15]])
        if DBG == "xq":
            for dt_i in range(DT):
                dma("gp", dbg_out.ap()[dt_i], xn[:, dt_i, :],
                    deps=[t_xq[3], sch.now("ve")])
        t_badd = None
        for dout in range(DT):
            t_badd = I("ve", "tensor_scalar", x[:, dout, :], x[:, dout, :],
                       b2q[:, dout:dout + 1], None, ALU.add,
                       deps=[t_b2], inc=True)
        pg = {i: None for i in range(4)}
        t_w1_load = t_w2_load = None
        t_ph1_last = t_ph2_last = None
        t_hq_chain = None
        for hc in range(4):
            dep1 = [t_ph1_last] + B0
            for k in range(DT):
                half, coff = hc // 2, (hc % 2) * 1024
                t_w1_load = dma("sq", w1buf[:, k, :],
                                s_w1q.ap()[k * 2 + half][:, coff:coff + 1024],
                                deps=[t_w1q[k * 2 + half]] + dep1)
            dep2 = [t_ph2_last] + B0
            for k in range(DT):
                jj, coff = hc * 4 + k // 2, (k % 2) * 1024
                t_w2_load = dma("sq", w2buf[:, k, :],
                                s_w2q.ap()[jj][:, coff:coff + 1024],
                                deps=[t_w2q[jj]] + dep2)
            for tt in range(8):
                ts0 = 256 * tt
                w = ts0 // 512
                se_toks = []
                for m in range(8):
                    pmi = m % 2
                    pm = p_mm[pmi]
                    for k in range(DT):
                        d = ([t_w1_load, t_xq[w], pg[pmi]] if k == 0 else ())
                        tok = mm(pm[:, :256],
                                 w1buf[:, k, m * 128:(m + 1) * 128],
                                 xn[:, k, ts0:ts0 + 256], k == 0, k == DT - 1,
                                 deps=d, inc=(k == DT - 1))
                    ht = hc * 8 + m
                    tse = I("se", "activation", kf[:, m * 256:(m + 1) * 256],
                            pm[:, :256], AF.Relu, bias=b1q[:, ht:ht + 1],
                            scale=1024.0, deps=[tok, t_b2, t_hq_chain],
                            inc=True)
                    pg[pmi] = tse
                    se_toks.append(tse)
                    t_ph1_last = tok
                I("ve", "tensor_copy", A["qi"][:, :2048], kf[:, :2048],
                  deps=se_toks + [G["q"]])
                I("ve", "tensor_scalar", A["qj"][:, :2048], A["qi"][:, :2048], 6, None,
                  ALU.arith_shift_right)
                I("ve", "tensor_scalar", A["qj"][:, :2048], A["qj"][:, :2048], 1, None,
                  ALU.bitwise_and)
                I("ve", "tensor_tensor", A["qi"][:, :2048], A["qi"][:, :2048],
                  A["qj"][:, :2048], ALU.add)
                I("ve", "tensor_scalar", A["qi"][:, :2048], A["qi"][:, :2048], 31, None,
                  ALU.add)
                I("ve", "tensor_scalar", A["qi"][:, :2048], A["qi"][:, :2048], 6, None,
                  ALU.arith_shift_right)
                t_hq_chain = I("ve", "tensor_scalar", hq[:, :2048],
                               A["qi"][:, :2048], 127.0, None, ALU.min,
                               deps=[t_ph2_last], inc=True)
                G["q"] = t_hq_chain
                if DBG == "hq" and hc == 0 and tt == 0:
                    dma("sq", dbg_out.ap()[2][:, 0:32], b1q[:, :],
                        deps=[t_b1])
                    dma("gp", dbg_out.ap()[0][:, 0:2048], hq[:, :2048],
                        deps=[t_hq_chain])
                    dma("sq", dbg_out.ap()[1][:, 0:2048], kf[:, :2048],
                        deps=[t_hq_chain])
                for dout in range(8):
                    pmi = 2 + dout % 2
                    pm = p_mm[pmi]
                    for k in range(DT):
                        d = ([t_w2_load, t_hq_chain, pg[pmi]] if k == 0 else ())
                        tok2 = mm(pm[:, :256],
                                  w2buf[:, k, dout * 128:(dout + 1) * 128],
                                  hq[:, k * 256:(k + 1) * 256],
                                  k == 0, k == DT - 1,
                                  deps=d, inc=(k == DT - 1))
                    tev = I("ve", "scalar_tensor_tensor",
                            x[:, dout, ts0:ts0 + 256], pm[:, :256], 0.5,
                            x[:, dout, ts0:ts0 + 256], ALU.mult, ALU.add,
                            deps=[tok2], inc=True)
                    pg[pmi] = tev
                    t_ph2_last = tok2
        return sch.now("ve")



    def moe(sc):
        B0 = bar()
        onesc = salloc(sc, "onesc", [128, 1], F32)
        ones8 = salloc(sc, "ones8", [128, 8], F32)
        gwbuf = salloc(sc, "gwbuf", [128, DT, 8], F32)
        se8 = salloc(sc, "se8", [1, 8], F32)
        s8b = salloc(sc, "s8b", [128, 8], F32)
        mr8 = salloc(sc, "mr8", [128, 8], F32)
        pgs = [salloc(sc, f"pg8_{q}", [128, 8], F32) for q in range(4)]
        lgs = [salloc(sc, f"lg8_{q}", [128, 8], F32) for q in range(4)]
        t8s = [salloc(sc, f"t8_{q}", [128, 8], F32) for q in range(4)]
        n1s = [salloc(sc, f"n1_{q}", [128, 1], F32) for q in range(4)]
        n2s = [salloc(sc, f"n2_{q}", [128, 1], F32) for q in range(4)]
        prs = [salloc(sc, f"pr8_{q}", [128, 8], F32) for q in range(4)]
        d8s = [salloc(sc, f"d8_{q}", [128, 1], F32) for q in range(4)]
        dds = [salloc(sc, f"dd8_{q}", [128, 8], F32) for q in range(4)]
        rcs = [salloc(sc, f"rec8_{q}", [128, 8], F32) for q in range(4)]
        mks = [salloc(sc, f"msk8_{q}", [128, 8], F32) for q in range(4)]
        cws = [salloc(sc, f"cw8_{q}", [128, 8], F32) for q in range(4)]
        sp32 = salloc(sc, "sp32", [128, 128], F32)
        cwT = salloc(sc, "cwT", [65, 3 * S], BF16)
        mkT = salloc(sc, "mkT", [65, 3 * S], BF16)
        t_on = I("ve", "memset", onesc[:, :], 1.0, deps=B0, inc=True)
        t_on8 = I("ve", "memset", ones8[:, :], 1.0, deps=B0, inc=True)
        t_sp0 = I("ve", "memset", sp32[:, :], 0.0, deps=B0, inc=True)
        for dt_i in range(DT):
            t_gw = dma("sq", gwbuf[:, dt_i, :], d_gw.ap()[dt_i], deps=B0)
        for dt_i in range(DT):
            tok_se = mm(p_st[0][0:1, 0:8], onesc[:, :], gwbuf[:, dt_i, :],
                        dt_i == 0, dt_i == DT - 1,
                        deps=[t_gw, t_on, G["pst"], B0[0]] if dt_i == 0 else (),
                        inc=(dt_i == DT - 1))
        t_se8 = I("ve", "tensor_copy", se8[:, :], p_st[0][0:1, 0:8],
                  deps=[tok_se], inc=True)
        G["pst"] = t_se8
        tok_sb = mm(p_st[1][:, 0:8], ones_f[0:1, :], se8[0:1, :], True, True,
                    deps=[t_se8, G.get("pst1")], inc=True)
        t_s8b = I("ve", "tensor_copy", s8b[:, :], p_st[1][:, 0:8],
                  deps=[tok_sb], inc=True)
        G["pst1"] = t_s8b
        G["pst"] = t_s8b

        t_xn = [None] * 4
        t_gT = None
        for w in range(4):
            st = ln_stats(w, deps=B0 + [T_X])
            tb0, tb1 = bcast2(st)
            last = None
            for dt_i in range(DT):
                xs = x[:, dt_i, 512 * w:512 * w + 512]
                I("ve", "tensor_tensor", lnpx[:, :], xs, p_bc[0][:, :],
                  ALU.subtract, deps=[tb0])
                last = I("ve", "tensor_tensor",
                         xn[:, dt_i, 512 * w:512 * w + 512],
                         lnpx[:, :], p_bc[1][:, :], ALU.mult, deps=[tb1],
                         inc=True)
            G["pbc"] = last
            t_xn[w] = last
            if DBG == "cw":
                dma("sq", dbg_out.ap()[5][0:1, w * 512:w * 512 + 512],
                    st["m"], deps=[st["t_m"]])
                dma("sq", dbg_out.ap()[6][0:1, w * 512:w * 512 + 512],
                    st["r"], deps=[st["t_r"]])
            tpgt = [None] * 4
            trmt = [None] * 4
            trrt = [None] * 4
            for tc4 in range(4):
                tc = w * 4 + tc4
                for dt_i in range(DT):
                    tok_a = mm(p_st[0][:, 0:8],
                               x[:, dt_i, tc * 128:tc * 128 + 128],
                               gwbuf[:, dt_i, :], dt_i == 0, dt_i == DT - 1,
                               deps=[t_gw, G["pst"], T_X] if dt_i == 0 else (),
                               inc=(dt_i == DT - 1))
                tpgt[tc4] = I("ve", "tensor_copy", pgs[tc4][:, :],
                              p_st[0][:, 0:8], deps=[tok_a], inc=True)
                G["pst"] = tpgt[tc4]
                ttrm = I("te", "transpose", p_st[0][:, 0:1],
                         st["m"][0:1, tc4 * 128:tc4 * 128 + 128],
                         ident[0:1, 0:1],
                         deps=[st["t_m"], T_ID, tpgt[tc4]], inc=True)
                trmt[tc4] = I("ve", "tensor_reduce", mr8[:, tc4:tc4 + 1],
                              p_st[0][:, 0:1], AX.X, ALU.max, negate=True,
                              deps=[ttrm], inc=True)
                G["pst"] = trmt[tc4]
                ttrr = I("te", "transpose", p_st[0][:, 0:1],
                         st["r"][0:1, tc4 * 128:tc4 * 128 + 128],
                         ident[0:1, 0:1],
                         deps=[st["t_r"], trmt[tc4]], inc=True)
                trrt[tc4] = I("ve", "tensor_reduce", mr8[:, 4 + tc4:5 + tc4],
                              p_st[0][:, 0:1], AX.X, ALU.max, deps=[ttrr],
                              inc=True)
                G["pst"] = trrt[tc4]
                G["rows"] = ttrr
            for q in range(4):
                I("ve", "scalar_tensor_tensor", lgs[q][:, :], s8b[:, :],
                  mr8[:, q:q + 1], pgs[q][:, :], ALU.mult, ALU.add,
                  deps=[t_s8b, trmt[q], G.get("dbgcw")])
            for q in range(4):
                I("ve", "tensor_scalar", lgs[q][:, :], lgs[q][:, :],
                  mr8[:, 4 + q:5 + q], None, ALU.mult, deps=[trrt[q]])
            for q in range(4):
                I("ve", "tensor_reduce", n1s[q][:, :], lgs[q][:, :], AX.X,
                  ALU.max, negate=True)
            for q in range(4):
                I("ve", "tensor_scalar", t8s[q][:, :], lgs[q][:, :],
                  n1s[q][:, 0:1], None, ALU.add)
            for q in range(4):
                I("ve", "tensor_scalar", t8s[q][:, :], t8s[q][:, :], 0.0,
                  None, ALU.is_ge)
            for q in range(4):
                I("ve", "scalar_tensor_tensor", t8s[q][:, :], t8s[q][:, :],
                  -1e30, lgs[q][:, :], ALU.mult, ALU.add)
            for q in range(4):
                I("ve", "tensor_reduce", n2s[q][:, :], t8s[q][:, :], AX.X,
                  ALU.max, negate=True)
            for q in range(4):
                I("ve", "tensor_scalar", mks[q][:, :], lgs[q][:, :],
                  n2s[q][:, 0:1], None, ALU.add)
            for q in range(4):
                I("ve", "tensor_scalar", mks[q][:, :], mks[q][:, :], 0.0,
                  None, ALU.is_ge, inc=True)
            vnow = sch.now("ve")
            tprs = [None] * 4
            for q in range(4):
                tprs[q] = I("se", "activation", prs[q][:, :], lgs[q][:, :],
                            AF.Exp, bias=n1s[q][:, :],
                            deps=[vnow] if q == 0 else (), inc=True)
            for q in range(4):
                I("ve", "tensor_reduce", d8s[q][:, :], prs[q][:, :], AX.X,
                  ALU.add, deps=[tprs[q]])
            for q in range(4):
                I("ve", "tensor_scalar", dds[q][:, :], ones8[:, :],
                  d8s[q][:, 0:1], None, ALU.mult, deps=[t_on8])
            for q in range(4):
                I("ve", "reciprocal", rcs[q][:, :], dds[q][:, :])
            for q in range(4):
                I("ve", "tensor_tensor", cws[q][:, :], prs[q][:, :],
                  mks[q][:, :], ALU.mult)
            tcwt = [None] * 4
            for q in range(4):
                tcwt[q] = I("ve", "tensor_tensor", cws[q][:, :], cws[q][:, :],
                            rcs[q][:, :], ALU.mult, inc=True)
            tmkt = [None] * 4
            for q in range(4):
                tmkt[q] = I("ve", "tensor_scalar", mks[q][:, :], mks[q][:, :],
                            0.5, None, ALU.mult, inc=True)
            if DBG == "cw":
                for q in range(4):
                    tc = w * 4 + q
                    G["dbglg"] = dma("sq", dbg_out.ap()[2][:, tc * 8:tc * 8 + 8],
                                     lgs[q][:, :], deps=[tcwt[q]])
                    dma("sq", dbg_out.ap()[3][:, tc * 8:tc * 8 + 8],
                        pgs[q][:, :], deps=[tpgt[q]])
                    dma("sq", dbg_out.ap()[7][:, 256 + tc * 8:256 + tc * 8 + 8],
                        prs[q][:, :], deps=[tprs[q]])
                    G["dbgcw"] = dma("sq",
                                     dbg_out.ap()[7][:, 640 + tc * 8:640 + tc * 8 + 8],
                                     cws[q][:, :], deps=[tcwt[q]])
            for tc4 in range(4):
                tc = w * 4 + tc4
                for half8 in range(2):
                    for srcb, dstT in ((cws[tc4], cwT), (mks[tc4], mkT)):
                        tspr = None
                        for e4 in range(4):
                            e = half8 * 4 + e4
                            tspr = I("ve", "tensor_reduce",
                                     sp32[:, e4 * 32:e4 * 32 + 1],
                                     srcb[:, e:e + 1], AX.X, ALU.max,
                                     deps=[tmkt[tc4], tcwt[tc4],
                                           G.get("sp")], inc=True)
                        ttr = I("te", "transpose", p_st[0][:, 0:128],
                                sp32[:, :], ident[:, :],
                                deps=[tspr, G["pst"], T_ID], inc=True)
                        G["sp"] = ttr
                        tcp = None
                        for e4 in range(4):
                            e = half8 * 4 + e4
                            rw3 = (e % 3) * 32
                            cb3 = (e // 3) * S
                            tcp = I("ve", "tensor_copy",
                                    dstT[rw3:rw3 + 1,
                                         cb3 + tc * 128:cb3 + tc * 128 + 128],
                                    p_st[0][e4 * 32:e4 * 32 + 1, 0:128],
                                    deps=[ttr], inc=True)
                        G["pst"] = tcp
                t_gT = tcp

        if DBG in ("cw", "moe1"):
            for dt_i in range(DT):
                tdx = dma("sq", dbg3.ap()[dt_i], x[:, dt_i, :], deps=B0)
                tdx = dma("gp", dbg2.ap()[dt_i], xn[:, dt_i, :],
                          deps=t_xn + [tdx])
            B0 = B0 + [tdx]
        if DBG == "cw":
            for e in range(8):
                rw3, cb3 = (e % 3) * 32, (e // 3) * S
                dma("gp", dbg_out.ap()[0][e:e + 1, :],
                    cwT[rw3:rw3 + 1, cb3:cb3 + S], deps=[t_gT])
                dma("gp", dbg_out.ap()[1][e:e + 1, :],
                    mkT[rw3:rw3 + 1, cb3:cb3 + S], deps=[t_gT])
        MPART = os.environ.get("KDEV_MPART", "all")
        sce = scope_open()
        w1b = salloc(sce, "mw1", [128, DT, 1024], BF16)
        w2b = salloc(sce, "mw2", [128, DT, 1024], BF16)
        w3b1 = salloc(sce, "mw3", [128, DT, 1024], BF16)
        w3b = [w3b1, w3b1]
        be = [salloc(sce, f"mbe{i}", [128, 3, 8], F32) for i in range(2)]
        x3m = salloc(sce, "mx3m", [128, DT, 512], BF16)
        hb = salloc(sce, "mhb", [128, DT, 256], BF16)
        ut = salloc(sce, "mut", [128, DT, 256], BF16)
        pb = salloc(sce, "mpb", [128, 256], BF16)
        cwb = salloc(sce, "mcwb", [128, 1024], BF16)
        mkb = salloc(sce, "mkb", [128, 1024], BF16)

        def stream_w(dsrc, e, buf, dep0):
            tok = None
            for dt_i in range(DT):
                tok = dma("gp", buf[:, dt_i, :], dsrc.ap()[e][dt_i],
                          deps=dep0 if dt_i == 0 else ())
            return tok

        t_y = None
        g_ub = None
        for half in (range(2) if MPART in ("all", "ge") else ()):
            th0 = half * 1024
            g_w = {"w1": None, "w2": None, "w3": None}
            t_w3 = {}
            t_w3[0] = stream_w(d_we3, 0, w3b[0], [sch.now("te")] + B0)
            g_be = [None, None]
            g_ps = {}
            t_x3m = None
            t_hb = None
            t_pv2 = None
            NEXP = int(os.environ.get("KDEV_NEXP", "8"))
            for e in range(NEXP):
                w3cur = w3b[e % 2]
                becur = be[e % 2]
                QTR = 2
                tbe = dma("sq", becur[:, 0, :], d_be1.ap()[e],
                          deps=[g_be[e % 2]])
                tbe = dma("sq", becur[:, 1, :], d_be2.ap()[e])
                tbe = dma("sq", becur[:, 2, :], d_be3.ap()[e])
                tcwb = tmkb = None
                rw, cb = (e % 3) * 32, (e // 3) * S
                for q2 in range(2):
                    tokc = mm(p_bc[0][:, :], ones[rw:rw + 1, :],
                              cwT[rw:rw + 1,
                                  cb + th0 + q2 * 512:cb + th0 + q2 * 512 + 512],
                              True, True,
                              deps=[t_gT, G.get("mbc0"), G["pbc"]], inc=True)
                    tcwb = I("ve", "tensor_copy",
                             cwb[:, q2 * 512:q2 * 512 + 512], p_bc[0][:, :],
                             deps=[tokc], inc=True)
                    G["mbc0"] = tcwb
                    tokm2 = mm(p_bc[1][:, :], ones[rw:rw + 1, :],
                               mkT[rw:rw + 1,
                                   cb + th0 + q2 * 512:cb + th0 + q2 * 512 + 512],
                               True, True,
                               deps=[t_gT, G.get("mbc1"), G["pbc"]], inc=True)
                    tmkb = I("ve", "tensor_copy",
                             mkb[:, q2 * 512:q2 * 512 + 512], p_bc[1][:, :],
                             deps=[tokm2], inc=True)
                    G["mbc1"] = tmkb
                t_w1 = stream_w(d_we1, e, w1b, [g_w["w1"]] + B0)
                t_w2 = stream_w(d_we2, e, w2b, [g_w["w2"]] + B0)
                for qtr in range(2):
                    tq0 = th0 + qtr * 512
                    for tt in range(2):
                        ts0 = tq0 + tt * 256
                        co = (qtr * 512 + tt * 256)
                        tu = None
                        for dt_i in range(DT):
                            tu = I("ve", "tensor_tensor", ut[:, dt_i, :],
                                   xn[:, dt_i, ts0:ts0 + 256],
                                   cwb[:, co:co + 256], ALU.mult,
                                   deps=[tcwb, t_xn[ts0 // 512], g_ub],
                                   inc=True)
                        for mt in range(DT):
                            pm = p_mm[mt % 2]
                            for dt_i in range(DT):
                                tok3 = mm(pm[:, :256],
                                          w3cur[:, dt_i, mt * 128:mt * 128 + 128],
                                          ut[:, dt_i, :], dt_i == 0,
                                          dt_i == DT - 1,
                                          deps=[tu, t_w3[e],
                                                g_ps.get(("e", mt % 2))]
                                          if dt_i == 0 else (),
                                          inc=(dt_i == DT - 1))
                            t_x3m = I("ve", "scalar_tensor_tensor",
                                      x3m[:, mt, tt * 256:tt * 256 + 256],
                                      pm[:, :256], becur[:, 2, mt:mt + 1],
                                      mkb[:, co:co + 256],
                                      ALU.add, ALU.mult,
                                      deps=[tok3, tbe, tmkb], inc=True)
                            g_ps[("e", mt % 2)] = t_x3m
                        g_ub = tok3
                    if qtr == 1 and e < 7:
                        t_w3[e + 1] = stream_w(d_we3, e + 1, w3b1, [tok3])
                    for tt in range(2):
                        ts0 = tq0 + tt * 256
                        co = (qtr * 512 + tt * 256)
                        tu = None
                        for dt_i in range(DT):
                            tu = I("ve", "tensor_tensor", ut[:, dt_i, :],
                                   xn[:, dt_i, ts0:ts0 + 256],
                                   cwb[:, co:co + 256], ALU.mult,
                                   deps=[tcwb, g_ub], inc=True)
                        for mt in range(DT):
                            pm = p_mm[mt % 2]
                            for dt_i in range(DT):
                                tok1 = mm(pm[:, :256],
                                          w1b[:, dt_i, mt * 128:mt * 128 + 128],
                                          ut[:, dt_i, :], dt_i == 0,
                                          dt_i == DT - 1,
                                          deps=[tu, t_w1,
                                                g_ps.get(("e", mt % 2))]
                                          if dt_i == 0 else (),
                                          inc=(dt_i == DT - 1))
                            t_hb = I("se", "activation", hb[:, mt, :],
                                     pm[:, :256], AF.Silu,
                                     bias=becur[:, 0, mt:mt + 1],
                                     deps=[tok1, tbe, g_ps.get("hb")],
                                     inc=True)
                            g_ps[("e", mt % 2)] = t_hb
                        g_ub = tok1
                        for dout in range(DT):
                            pm = p_mm[2 + dout % 2]
                            for k in range(DT):
                                tok2 = mm(pm[:, :256],
                                          w2b[:, k, dout * 128:dout * 128 + 128],
                                          hb[:, k, :], k == 0, k == DT - 1,
                                          deps=[t_hb, t_w2,
                                                g_ps.get(("o", dout % 2))]
                                          if k == 0 else (),
                                          inc=(k == DT - 1))
                            g_ps["hb"] = tok2
                            tpp = I("ve", "scalar_tensor_tensor", pb[:, :],
                                    pm[:, :256], becur[:, 1, dout:dout + 1],
                                    x3m[:, dout, tt * 256:tt * 256 + 256],
                                    ALU.add, ALU.mult,
                                    deps=[tok2, t_x3m], inc=True)
                            t_pv2 = I("ve", "tensor_tensor",
                                      x[:, dout, ts0:ts0 + 256],
                                      x[:, dout, ts0:ts0 + 256],
                                      pb[:, :], ALU.add, deps=[T_X],
                                      inc=True)
                            g_ps[("o", dout % 2)] = t_pv2
                g_w["w3"] = tok3
                g_w["w1"] = tok1
                g_w["w2"] = tok2
                g_be[e % 2] = t_pv2
                if DBG == "moe1" and e == 0 and half == 0:
                    dv = sch.now("ve")
                    dma("gp", dbg_out.ap()[0][:, 0:1024], cwb[:, :], deps=[dv])
                    dma("gp", dbg_out.ap()[0][:, 1024:2048], mkb[:, :])
                    for j in range(4):
                        dma("gp", dbg_out.ap()[1 + j // 2]
                            [:, (j % 2) * 1024:(j % 2) * 1024 + 1024],
                            x3m[:, 2 * j:2 * j + 2, :])
                    dma("gp", dbg_out.ap()[3], hb[:, :, :])
                    dma("gp", dbg_out.ap()[4], ut[:, :, :])
                    for j in range(3):
                        dma("gp", dbg_out.ap()[5][:, j * 8:j * 8 + 8],
                            becur[:, j, :])
                    dma("gp", dbg_out.ap()[6][:, 0:1024], w1b[:, 0, :])
                    dma("gp", dbg_out.ap()[6][:, 1024:2048], w2b[:, 0, :])
                    dma("gp", dbg_out.ap()[7][:, 0:1024], w3cur[:, 0, :])
        scope_close()

        scs = scope_open()
        sw1b = salloc(scs, "msw1", [128, DT, 1024], BF16)
        sw2b = salloc(scs, "msw2", [128, DT, 1024], BF16)
        sw3b = salloc(scs, "msw3", [128, DT, 1024], BF16)
        h1b = salloc(scs, "mh1", [128, DT, 512], BF16)
        gb = salloc(scs, "mgb", [128, DT, 512], BF16)
        bsh = bar()
        g_sw = {}
        t_sh = None
        for ih in (range(2) if MPART in ("all", "gs") else ()):
            tok_s1 = tok_s2 = tok_s3 = None
            for dt_i in range(DT):
                tok_s1 = dma("gp", sw1b[:, dt_i, :],
                             d_sw1.ap()[dt_i][:, ih * 1024:ih * 1024 + 1024],
                             deps=[g_sw.get("s1")] + bsh if dt_i == 0 else ())
                tok_s2 = dma("gp", sw2b[:, dt_i, :],
                             d_sw2.ap()[dt_i][:, ih * 1024:ih * 1024 + 1024],
                             deps=[g_sw.get("s2")] if dt_i == 0 else ())
                tok_s3 = dma("gp", sw3b[:, dt_i, :],
                             d_sw3.ap()[ih * 8 + dt_i],
                             deps=[g_sw.get("s3")] if dt_i == 0 else ())
            for tt in range(4):
                ts0 = tt * 512
                for mt in range(DT):
                    pmA = p_mm[0 + mt % 2]
                    pmB = p_mm[2 + mt % 2]
                    for dt_i in range(DT):
                        ta = mm(pmA[:, :512],
                                sw1b[:, dt_i, mt * 128:mt * 128 + 128],
                                xn[:, dt_i, ts0:ts0 + 512],
                                dt_i == 0, dt_i == DT - 1,
                                deps=[tok_s1, t_xn[ts0 // 512],
                                      g_sw.get(("bank", mt % 2))]
                                if dt_i == 0 else (),
                                inc=(dt_i == DT - 1))
                    for dt_i in range(DT):
                        tb2 = mm(pmB[:, :512],
                                 sw2b[:, dt_i, mt * 128:mt * 128 + 128],
                                 xn[:, dt_i, ts0:ts0 + 512],
                                 dt_i == 0, dt_i == DT - 1,
                                 deps=[tok_s2, g_sw.get(("bank", 2 + mt % 2))]
                                 if dt_i == 0 else (),
                                 inc=(dt_i == DT - 1))
                    th1 = I("se", "activation", h1b[:, mt, :], pmA[:, :512],
                            AF.Silu, deps=[ta, g_sw.get(("hb", mt))],
                            inc=True)
                    g_sw[("bank", mt % 2)] = th1
                    tg = I("ve", "tensor_tensor", gb[:, mt, :], h1b[:, mt, :],
                           pmB[:, :512], ALU.mult,
                           deps=[th1, tb2, g_sw.get("gbg")], inc=True)
                    g_sw[("bank", 2 + mt % 2)] = tg
                    g_sw[("hb", mt)] = tg
                    t_gb = tg
                for dout in range(DT):
                    pm = p_mm[dout % 4]
                    for k in range(DT):
                        tko = mm(pm[:, :512],
                                 sw3b[:, k, dout * 128:dout * 128 + 128],
                                 gb[:, k, :], k == 0, k == DT - 1,
                                 deps=[t_gb, tok_s3, g_sw.get(("bank", dout % 4))]
                                 if k == 0 else (),
                                 inc=(k == DT - 1))
                    g_sw["gbg"] = tko
                    t_sh = I("ve", "scalar_tensor_tensor",
                             x[:, dout, ts0:ts0 + 512], pm[:, :512], 0.5,
                             x[:, dout, ts0:ts0 + 512], ALU.mult, ALU.add,
                             deps=[tko], inc=True)
                    g_sw[("bank", dout % 4)] = t_sh
            g_sw["s1"] = tko
            g_sw["s2"] = tko
            g_sw["s3"] = tko
        scope_close()
        return sch.now("ve")

    def attn(layer):
        B0 = bar()
        nh = 16 if layer == 0 else 8
        nhp = nh // 2
        GH = 4
        ngrp = nhp // GH
        wqkv = d_wqkv0 if layer == 0 else d_wqkv1
        d_wo = d_wo0 if layer == 0 else d_wo1
        nct = DT if layer == 0 else 4
        EPSD = 0.0 if layer == 0 else 1e-10

        sca = scope_open()
        gbe = None
        t_gbe = None
        if layer == 1:
            gbe = salloc(sca, "gbe", [128, 16], F32)
            t_gbe = dma("sq", gbe[:, 0:8], d_g1b1.ap()[0], deps=B0)
            t_gbe = dma("sq", gbe[:, 8:16], d_g1b1.ap()[1], deps=B0)
        t_xn = [None] * 4
        for w in range(4):
            st = ln_stats(w, deps=B0 + [T_X])
            tb0, tb1 = bcast2(st)
            last = None
            for dt_i in range(DT):
                xs = x[:, dt_i, 512 * w:512 * w + 512]
                I("ve", "tensor_tensor", lnpx[:, :], xs, p_bc[0][:, :],
                  ALU.subtract, deps=[tb0])
                last = I("ve", "tensor_tensor", xn[:, dt_i, 512 * w:512 * w + 512],
                         lnpx[:, :], p_bc[1][:, :], ALU.mult, deps=[tb1],
                         inc=True)
            G["pbc"] = last
            if layer == 1:
                if DBG == "a1pre":
                    tpd = None
                    for dt_i in range(DT):
                        tpd = dma("gp", dbg2.ap()[dt_i][:, 512 * w:512 * w + 512],
                                  xn[:, dt_i, 512 * w:512 * w + 512],
                                  deps=[last, tpd])
                    last = tpd
                st2 = ln_stats(w, deps=[last],
                               src_bf=lambda dt_i: xn[:, dt_i, 512 * w:512 * w + 512])
                tb0, tb1 = bcast2(st2)
                for dt_i in range(DT):
                    xsl = xn[:, dt_i, 512 * w:512 * w + 512]
                    I("ve", "tensor_tensor", lnpx[:, :], xsl, p_bc[0][:, :],
                      ALU.subtract, deps=[tb0, last])
                    tq2 = I("ve", "tensor_tensor", lnpx[:, :], lnpx[:, :],
                            p_bc[1][:, :], ALU.mult, deps=[tb1], inc=True)
                    last = I("se", "activation", xsl, lnpx[:, :], AF.Identity,
                             scale=gbe[:, dt_i:dt_i + 1],
                             bias=gbe[:, 8 + dt_i:9 + dt_i],
                             deps=[tq2, t_gbe], inc=True)
                G["pbc"] = last
            t_xn[w] = last
        t_xn_all = [t for t in t_xn]
        if DBG in ("a1", "a1pre"):
            for dt_i in range(DT):
                tdx = dma("sq", dbg3.ap()[dt_i], x[:, dt_i, :],
                          deps=B0 + t_xn)
                G["dbgx"] = tdx
                tdx = dma("gp", dbg2.ap()[dt_i], xn[:, dt_i, :],
                          deps=t_xn + [tdx])

        ctx = [salloc(sca, f"actx{i}", [128, S], BF16) for i in range(nhp)]
        mtmp = salloc(sca, "amtmp", [128, 8], F32)
        mfin = salloc(sca, "amfin", [128, 8], F32)
        mrec = rows[0:1, 0:512]
        ct64 = salloc(sca, "act64", [64, 512], BF16)
        bq8 = bk8 = bvr = bo8 = None
        if layer == 0:
            bq8 = salloc(sca, "abq", [128, 64], F32)
            bk8 = salloc(sca, "abk", [128, 64], F32)
            bvr = salloc(sca, "abvr", [1, 1024], BF16)
            t_bq = I("ve", "memset", bq8[:, :], 0.0, deps=B0, inc=True)
            t_bq = dma("sq", bq8[:, 0:8], d_bqk0.ap()[0], deps=B0 + [t_bq])
            t_bq = I("ve", "tensor_scalar", bq8[:, 0:64], bq8[:, 0:64], 0.125,
                     None, ALU.mult, deps=[t_bq], inc=True)
            t_bk = dma("sq", bk8[:, 0:8], d_bqk0.ap()[1], deps=B0)
            t_bv = dma("gp", bvr[:, :], d_bv0.ap(), deps=B0)
            bo8 = salloc(sca, "abo", [128, 8], F32)
            t_bo = dma("sq", bo8[:, :], d_bo0.ap(), deps=B0)
        else:
            t_bq = t_bk = t_bv = t_bo = None

        LIMIT = os.environ.get("KDEV_ALIM", "all")
        t_proj_done = []
        t_score_done = []
        G_ctx = {}

        scg = scope_open()
        Qg = [salloc(scg, f"aqg{i}", [128, S], BF16) for i in range(GH)]
        Kg = [salloc(scg, f"akg{i}", [128, S], BF16) for i in range(GH)]
        Vg = salloc(scg, "avg", [128, 16, 520], BF16)
        for g in range(ngrp):
            tbar_te = sch.now("te")
            scw = scope_open()
            wbuf = salloc(scw, "awbuf", [128, 8, 512], BF16)
            qoff = g * 512
            pgq = {}
            t_evac = {0: None, 1: None}

            def proj_qk2(sel, dest, scale, bias8):
                tok_w = None
                tprev = sch.now("te")
                for dt_i in range(DT):
                    src_ap = wqkv.ap()[sel][dt_i]
                    if layer == 0:
                        src_ap = src_ap[:, qoff:qoff + 512]
                    tok_w = dma("gp", wbuf[:, dt_i, :], src_ap,
                                deps=[t_evac[0], t_evac[1], tprev])
                tev = None
                for hl in range(GH):
                    hp = g * GH + hl
                    for tw in range(4):
                        pi = (hl * 4 + tw) % 2
                        pm = p_mm[pi]
                        dep0 = [tok_w, t_xn[tw], pgq.get(pi), tbar_te]
                        for dt_i in range(DT):
                            tokm = mm(pm[:, :],
                                      wbuf[:, dt_i, hl * 128:(hl + 1) * 128],
                                      xn[:, dt_i, tw * 512:tw * 512 + 512],
                                      dt_i == 0, dt_i == DT - 1,
                                      deps=dep0 if dt_i == 0 else (),
                                      inc=(dt_i == DT - 1))
                        kw = {}
                        if bias8 is not None:
                            kw["bias"] = bias8[:, hp:hp + 1]
                        tev = I("se", "activation",
                                dest[hl][:, tw * 512:tw * 512 + 512], pm[:, :],
                                AF.Identity, scale=scale,
                                deps=[tokm, t_bq, t_bk], inc=True, **kw)
                        pgq[pi] = tev
                        t_evac[pi] = tev
                return tev

            tQ = proj_qk2(0, Qg, 0.125, bq8)
            tK = proj_qk2(1, Kg, 1.0, bk8)
            tvm = I("ve", "memset", Vg[:, :, :], 1.0,
                    deps=B0 + [tbar_te], inc=True)
            tok_w = None
            tprev = sch.now("te")
            for dt_i in range(DT):
                src_ap = wqkv.ap()[2][dt_i]
                if layer == 0:
                    src_ap = src_ap[:, qoff:qoff + 512]
                tok_w = dma("gp", wbuf[:, dt_i, :], src_ap,
                            deps=[t_evac[0], t_evac[1], tprev])
            t_vev = {0: None, 1: None}
            for tc in range(16):
                pi = tc % 2
                pm = p_mm[pi]
                dep0 = [tok_w, t_xn[tc // 4], t_vev[pi]]
                for dt_i in range(DT):
                    tokm = mm(pm[:, :], xn[:, dt_i, tc * 128:tc * 128 + 128],
                              wbuf[:, dt_i, :], dt_i == 0,
                              (dt_i == DT - 1) and layer != 0,
                              deps=dep0 if dt_i == 0 else (),
                              inc=(dt_i == DT - 1) and layer != 0)
                if layer == 0:
                    tokm = mm(pm[:, :], ones[0:1, 0:128],
                              bvr[0:1, qoff:qoff + 512],
                              False, True, deps=[t_bv], inc=True)
                tev = None
                for hl8 in range(8):
                    tev = I("ve", "tensor_copy",
                            Vg[:, tc, hl8 * 65:hl8 * 65 + 64],
                            pm[:, hl8 * 64:hl8 * 64 + 64],
                            deps=[tokm, tvm], inc=True)
                t_vev[pi] = tev
            t_proj_done.append(tev)

            scope_close()
            if LIMIT == "proj":
                continue

            scb = scope_open()
            Kxt = salloc(scb, "akxt", [96, S], BF16)
            Qxt = salloc(scb, "aqxt", [96, S], BF16)
            ptst = [salloc(scb, f"aptst{i}", [128, 512], BF16) for i in range(2)]
            mall = salloc(scb, "amall", [128, 32], F32)
            mall32 = salloc(scb, "amall32", [128, 128], F32)
            t_m32z = I("ve", "memset", mall32[:, :], 0.0, deps=B0, inc=True)
            maskB = salloc(scb, "amaskB", [128, 896], BF16)
            t_mb = dma("gp", maskB[:, :], (d_maskB0 if layer == 0 else d_maskB1).ap(),
                       deps=B0)
            maskA = None
            t_ma = None
            if layer == 1:
                maskA = salloc(scb, "amaskA", [128, 896], BF16)
                t_ma = dma("gp", maskA[:, :], d_maskA1.ap(), deps=B0)

            G_k = {}
            vbar = sch.now("ve")
            G_k[("A", 0)] = vbar
            G_k[("A", 1)] = vbar
            t_exp = {0: vbar, 1: vbar}
            t_pv = {0: None, 1: None}
            t_pairdone = None
            t_headdone = None
            NHL = int(os.environ.get("KDEV_NHL", str(GH)))
            NSP = int(os.environ.get("KDEV_NSP", "4"))
            for hl in range(min(GH, NHL) if LIMIT != "setup" else 0):
                hp = g * GH + hl
                for s in range(NSP):
                    for qtl in range(4):
                        qt = s * 4 + qtl
                        nch = s + 1
                        for c in range(nch):
                            par = c % 2
                            pA = p_mm[0 + 2 * par]
                            pB = p_mm[1 + 2 * par]
                            depq = [tQ, tK, G_k.get(("A", par)),
                                    t_exp[0], t_exp[1]]
                            for (rg, cg) in ((0, 0), (0, 64), (64, 0), (64, 64)):
                                po = pA if rg == 0 else pB
                                tokq = mm(po[cg:cg + 64, :],
                                          Qg[hl][rg:rg + 64,
                                                 qt * 128 + cg:qt * 128 + cg + 64],
                                          Kg[hl][rg:rg + 64, c * 512:c * 512 + 512],
                                          True, True,
                                          deps=depq if (rg == 0 and cg == 0) else (),
                                          inc=(rg == 64 and cg == 64),
                                          tile_position=(rg, cg))
                            if LIMIT == "pq":
                                continue
                            if layer == 1 and c == s:
                                u = qt % 4
                                msl = maskA[:, 384 - 128 * u:896 - 128 * u]
                                I("ve", "tensor_tensor", pA[:, :], pA[:, :],
                                  msl, ALU.add, deps=[tokq, t_ma])
                                tokq = I("ve", "tensor_tensor", pB[:, :],
                                         pB[:, :], msl, ALU.add, inc=True)
                            wext = qt * 128 + 128 - c * 512
                            apA = pA[:, 0:wext] if wext < 512 else pA[:, :]
                            apB = pB[:, 0:wext] if wext < 512 else pB[:, :]
                            ta = I("ve", "tensor_reduce", mtmp[:, c:c + 1],
                                   apA, AX.X, ALU.max, deps=[tokq], inc=True)
                            tb = I("ve", "tensor_reduce", mtmp[:, 4 + c:5 + c],
                                   apB, AX.X, ALU.max, deps=[tokq], inc=True)
                            G_k[("A", par)] = tb
                        if LIMIT == "pq":
                            continue
                        I("ve", "tensor_reduce", mall[:, qt:qt + 1],
                          mtmp[:, 0:nch], AX.X, ALU.max, negate=True,
                          deps=[G_k.get("mfin")], inc=True)
                        G_k["lastred"] = I(
                            "ve", "tensor_reduce", mall[:, 16 + qt:17 + qt],
                            mtmp[:, 4:4 + nch], AX.X, ALU.max, negate=True,
                            inc=True)
                t_A_done = G_k["lastred"]
                if LIMIT in ("pa", "pq"):
                    continue
                for hsel in range(2):
                    h_local8 = hl * 2 + hsel
                    db = [t_headdone, tK, tQ]
                    t_z = I("ve", "memset", Kxt[64:96, :], 0.0, deps=db,
                            inc=True)
                    t_z = I("ve", "memset", Kxt[64:65, :], 1.0, inc=True)
                    t_z = I("ve", "memset", Qxt[64:96, :], 0.0, inc=True)
                    tkx = I("ve", "tensor_copy", Kxt[0:64, :],
                            Kg[hl][hsel * 64:hsel * 64 + 64, :], deps=db,
                            inc=True)
                    tqx = I("ve", "tensor_copy", Qxt[0:64, :],
                            Qg[hl][hsel * 64:hsel * 64 + 64, :], deps=db,
                            inc=True)
                    pctx = p_bc[0][0:65, :]
                    for s in range(4):
                        qs = s * 512
                        tsp = None
                        for u in range(4):
                            qt = s * 4 + u
                            tsp = I("ve", "tensor_reduce",
                                    mall32[:, u * 32:u * 32 + 1],
                                    mall[:, hsel * 16 + qt:hsel * 16 + qt + 1],
                                    AX.X, ALU.max,
                                    deps=[t_A_done, G_k.get("m32"), t_m32z],
                                    inc=True)
                        ttr = I("te", "transpose", p_st[0][:, 0:128],
                                mall32[:, :], ident[:, :],
                                deps=[tsp, T_ID, G_k.get("tr")], inc=True)
                        G_k["m32"] = ttr
                        tf = None
                        for u in range(4):
                            tf = I("ve", "tensor_copy",
                                   Qxt[64:65, qs + u * 128:qs + (u + 1) * 128],
                                   p_st[0][u * 32:u * 32 + 1, 0:128],
                                   deps=[ttr, tqx, G_k.get("qrow")], inc=True)
                        G_k["tr"] = tf
                        G_k["mfin"] = ttr
                        if LIMIT == "pfill":
                            continue
                        nkc = 4 * (s + 1)
                        for kc in range(nkc):
                            par = kc % 2
                            pm = p_mm[2 + par]
                            depb = [tf, tkx, t_z, G_k.get(("B", par)),
                                    t_exp[par]]
                            tokb = mm(pm[:, :],
                                      Kxt[0:96, kc * 128:kc * 128 + 128],
                                      Qxt[0:96, qs:qs + 512],
                                      True, True, deps=depb, inc=True)
                            if kc >= 4 * s:
                                u = kc - 4 * s
                                tokb = I("ve", "tensor_tensor", pm[:, :],
                                         pm[:, :],
                                         maskB[:, 384 - 128 * u:896 - 128 * u],
                                         ALU.add, deps=[tokb, t_mb], inc=True)
                            texp = I("se", "activation", ptst[par][:, :],
                                     pm[:, :], AF.Exp,
                                     deps=[tokb, t_pv[par]], inc=True)
                            t_exp[par] = texp
                            G_k[("B", par)] = texp
                            tpv = mm(pctx, Vg[:, kc, h_local8 * 65:
                                              h_local8 * 65 + 65],
                                     ptst[par][:, :], kc == 0, kc == nkc - 1,
                                     deps=[texp, t_proj_done[-1],
                                           G_ctx.get(0), G["pbc"]]
                                     if kc == 0 else [texp], inc=True)
                            t_pv[par] = tpv
                            t_pairdone = tokb
                            t_headdone = tokb
                        if LIMIT == "pb":
                            continue
                        if EPSD != 0.0:
                            trc = I("ve", "tensor_scalar", mrec[0:1, :],
                                    pctx[64:65, :], EPSD, None, ALU.add,
                                    deps=[tpv], inc=True)
                            trc = I("ve", "reciprocal", mrec[0:1, :],
                                    mrec[0:1, :], inc=True)
                        else:
                            trc = I("ve", "reciprocal", mrec[0:1, :],
                                    pctx[64:65, :], deps=[tpv], inc=True)
                        tev = I("se", "activation", ct64[:, :], pctx[0:64, :],
                                AF.Copy, deps=[tpv, G_k.get("rb")], inc=True)
                        tbc = mm(p_st[1][0:64, :], ones_f[0:1, 0:64],
                                 mrec[0:1, :], True, True,
                                 deps=[trc, G_k.get("rb")], inc=True)
                        tw2 = I("ve", "tensor_tensor",
                                ctx[hp][hsel * 64:hsel * 64 + 64, qs:qs + 512],
                                ct64[:, :], p_st[1][0:64, :], ALU.mult,
                                deps=[tbc, tev], inc=True)
                        G_k["rb"] = tw2
                        G_ctx[0] = tw2
                        G_k["qrow"] = tokb
                        t_score_done.append(tw2)
            scope_close()

        scope_close()
        if LIMIT in ("proj", "scores"):
            scope_close()
            return sch.now("ve")
        scw2 = scope_open()
        wob = salloc(scw2, "awob", [128, nct, 1024], BF16)
        t_wo = None
        bwo = bar()
        for ct_i in range(nct):
            t_wo = dma("gp", wob[:, ct_i, :], d_wo.ap()[ct_i],
                       deps=bwo if ct_i == 0 else ())
        t_badd = None
        if layer == 0:
            for dout in range(DT):
                t_badd = I("ve", "tensor_scalar", x[:, dout, :], x[:, dout, :],
                           bo8[:, dout:dout + 1], None, ALU.add,
                           deps=[t_bo] + t_score_done[-2:], inc=True)
        dep_sc = t_score_done[-2:]
        pgo = {}
        for dout in range(DT):
            for tw in range(4):
                pi = (dout * 4 + tw) % 4
                pm = p_mm[pi]
                for ct_i in range(nct):
                    toko = mm(pm[:, :], wob[:, ct_i, dout * 128:(dout + 1) * 128],
                              ctx[ct_i][:, tw * 512:tw * 512 + 512],
                              ct_i == 0, ct_i == nct - 1,
                              deps=([t_wo, pgo.get(pi)] + dep_sc)
                              if ct_i == 0 else (),
                              inc=(ct_i == nct - 1))
                tevo = I("ve", "scalar_tensor_tensor",
                         x[:, dout, tw * 512:tw * 512 + 512], pm[:, :], 1.0,
                         x[:, dout, tw * 512:tw * 512 + 512], ALU.mult,
                         ALU.add, deps=[toko, t_badd, G.get("dbgx")],
                         inc=True)
                pgo[pi] = tevo
        scope_close()
        scope_close()
        return sch.now("ve")

    l0ffn(sc0)
    scope_close()
    if n_sub >= 2:
        attn(0)
    if n_sub >= 3:
        sc2 = scope_open()
        alloc_arena(sc2)
        l0ffn(sc2)
        scope_close()
    if n_sub >= 4:
        sc3 = scope_open()
        moe(sc3)
        scope_close()
    if n_sub >= 5:
        attn(1)
    if n_sub >= 6:
        sc5 = scope_open()
        moe(sc5)
        scope_close()
    t_done = sch.now("ve")

    for dt_i in range(DT):
        dma("sq", x_out.ap()[dt_i], x[:, dt_i, :], deps=[t_done])

    def fin(e, n=sch.cnt["sq"], sem=sch.sem["sq"]):
        e.wait_ge(sem, n)
    sch.ops["sq"].append(fin)

    sch.finalize()
    return nc


def stage_weights(inputs):
    w1 = np.asarray(inputs["L0_ffn_w1"], dtype=np.float32)
    w2 = np.asarray(inputs["L0_ffn_w2"], dtype=np.float32)
    t1 = np.ascontiguousarray(
        w1.T.reshape(DT, 128, 2, 2048).transpose(0, 2, 1, 3).reshape(16, 128, 2048))
    t2 = np.ascontiguousarray(
        w2.T.reshape(32, 128, 1024).reshape(16, 2, 128, 1024)
        .transpose(0, 2, 1, 3).reshape(16, 128, 2048))
    b1 = np.ascontiguousarray(
        np.asarray(inputs["L0_ffn_b1"], np.float32).reshape(32, 128).T)
    b2 = np.ascontiguousarray(
        np.asarray(inputs["L0_ffn_b2"], np.float32).reshape(8, 128).T)
    out = {"d_w1": t1, "d_w2": t2, "d_b1": b1, "d_b2": b2}

    Wq = np.asarray(inputs["L0_attn_Wq"], np.float32)
    Wk = np.asarray(inputs["L0_attn_Wk"], np.float32)
    Wv = np.asarray(inputs["L0_attn_Wv"], np.float32)
    qkv0 = np.stack([w.transpose(1, 0, 2).reshape(D, 1024).reshape(DT, 128, 1024)
                     for w in (Wq, Wk, Wv)], 0)
    out["d_wqkv0"] = np.ascontiguousarray(qkv0)
    bq = np.asarray(inputs["L0_attn_bq"], np.float32).reshape(8, 128).T
    bk = np.asarray(inputs["L0_attn_bk"], np.float32).reshape(8, 128).T
    out["d_bqk0"] = np.ascontiguousarray(np.stack([bq, bk], 0))
    out["d_bv0"] = np.ascontiguousarray(
        np.asarray(inputs["L0_attn_bv"], np.float32).reshape(1, 1024))
    out["d_wo0"] = np.ascontiguousarray(
        np.asarray(inputs["L0_attn_Wo"], np.float32).reshape(DT, 128, 1024))
    out["d_bo0"] = np.ascontiguousarray(
        np.asarray(inputs["L0_attn_bo"], np.float32).reshape(8, 128).T)
    jj = np.arange(896)[None, :]
    pp = np.arange(128)[:, None]
    out["d_maskB0"] = np.where(jj < pp + 384, np.float32(-1e9),
                               np.float32(0)).astype(np.float32)
    wq1 = np.asarray(inputs["L1_attn_w_qkv"], np.float32)
    qkv1 = np.stack([wq1[:, s * 512:(s + 1) * 512].reshape(DT, 128, 512)
                     for s in range(3)], 0)
    out["d_wqkv1"] = np.ascontiguousarray(qkv1)
    out["d_wo1"] = np.ascontiguousarray(
        np.asarray(inputs["L1_attn_w_out"], np.float32).reshape(4, 128, 1024))
    g1 = np.asarray(inputs["L1_attn_ln_gamma"], np.float32).reshape(8, 128).T
    be1 = np.asarray(inputs["L1_attn_ln_beta"], np.float32).reshape(8, 128).T
    out["d_g1b1"] = np.ascontiguousarray(np.stack([g1, be1], 0))
    out["d_maskA1"] = np.where(jj > pp + 384, np.float32(-1e38),
                               np.float32(0)).astype(np.float32)
    out["d_maskB1"] = np.where(jj < pp + 384, np.float32(-1e38),
                               np.float32(0)).astype(np.float32)
    gw = np.asarray(inputs["L1_ffn_gate_w"], np.float32)
    out["d_gw"] = np.ascontiguousarray(gw.T.reshape(DT, 128, 8))
    for nm, key in (("d_we1", "L1_ffn_w1"), ("d_we2", "L1_ffn_w2"),
                    ("d_we3", "L1_ffn_w3")):
        wv = np.asarray(inputs[key], np.float32)
        out[nm] = np.ascontiguousarray(
            wv.transpose(0, 2, 1).reshape(8, DT, 128, 1024))
    for nm, key in (("d_be1", "L1_ffn_b1"), ("d_be2", "L1_ffn_b2"),
                    ("d_be3", "L1_ffn_b3")):
        bv = np.asarray(inputs[key], np.float32)
        out[nm] = np.ascontiguousarray(
            bv.reshape(8, 8, 128).transpose(0, 2, 1))
    out["d_sw1"] = np.ascontiguousarray(
        np.asarray(inputs["L1_ffn_sw1"], np.float32).T.reshape(DT, 128, 2048))
    out["d_sw2"] = np.ascontiguousarray(
        np.asarray(inputs["L1_ffn_sw2"], np.float32).T.reshape(DT, 128, 2048))
    out["d_sw3"] = np.ascontiguousarray(
        np.asarray(inputs["L1_ffn_sw3"], np.float32).T.reshape(16, 128, 1024))
    return out


def kernel(**inputs):
    nc = build_program(N_SUB)
    shared = stage_weights(inputs)
    xfull = np.asarray(inputs["L0_attn_embeddings"], np.float32)
    in_maps = []
    for c in range(NCORE):
        xfm = np.ascontiguousarray(xfull[c].T.reshape(DT, 128, S))
        im = {"x_in": xfm}
        im.update(shared)
        in_maps.append(im)
    res = run_bass_kernel_spmd(nc, in_maps, core_ids=list(range(NCORE)))
    outs = []
    for c in range(NCORE):
        o = np.asarray(res.results[c]["x_out"])
        outs.append(o.transpose(2, 0, 1).reshape(S, D))
    return np.stack(outs, 0)


def kernel_debug(inputs):
    nc = build_program(N_SUB)
    shared = stage_weights(inputs)
    xfull = np.asarray(inputs["L0_attn_embeddings"], np.float32)
    im = {"x_in": np.ascontiguousarray(xfull[0].T.reshape(DT, 128, S))}
    im.update(shared)
    res = run_bass_kernel_spmd(nc, [im], core_ids=[0])
    return {k: np.asarray(v) for k, v in res.results[0].items()}
